# version 06
# baseline (speedup 1.0000x reference)
import contextlib
import math

import numpy as np
import ml_dtypes

import concourse.bass as bass
import concourse.mybir as mybir
from concourse.bass_utils import run_bass_kernel_spmd
from concourse.alu_op_type import AluOpType as OP

F32 = mybir.dt.float32
BF16 = mybir.dt.bfloat16
I32 = mybir.dt.int32
U8 = mybir.dt.uint8
AF = mybir.ActivationFunctionType
AX = mybir.AxisListType.X
BF = ml_dtypes.bfloat16

N_CORES = 8
D = 2048
MLP = 16384
B = 2
T = 4096
TC = (B * T) // N_CORES
H = 16
DK = 128
DV = 128
M = 64
HPC = H // N_CORES * B
EPS = 1e-5
CLAMP = 32.0
SCALE = float(DK) ** -0.5
GELU_B = 0.7978845608028654
GELU_C = 0.044715



class Sem:
    def __init__(self, name):
        self.name = name
        self.h = None
        self.n = 0


class Buf:
    __slots__ = ("writer", "readers", "name", "bank")

    def __init__(self, name="", bank=None):
        self.writer = None
        self.readers = []
        self.name = name
        self.bank = bank


class Bank:
    __slots__ = ("pe_tag", "readers")

    def __init__(self):
        self.pe_tag = None
        self.readers = []


class Eng:
    def __init__(self, ctx, name, sem, is_dma_queue=False, sem_pool=None):
        self.ctx = ctx
        self.name = name
        self.sem = sem
        self.sem_pool = sem_pool
        self.rr = 0
        self.is_dma_queue = is_dma_queue
        self.items = []
        self.satisfied = {}

    def wait(self, sem, count, allow_self=True):
        if count <= 0:
            return
        if sem is self.sem and self.name == "pe":
            return
        if sem is self.sem and not allow_self:
            return
        if self.satisfied.get(sem, 0) >= count:
            return
        self.items.append(("wait", (sem, count)))
        self.satisfied[sem] = count


class Ctx:

    def __init__(self, name):
        self.nc = bass.Bass("TRN2", target_bir_lowering=False)
        self.name = name
        self.stack = contextlib.ExitStack()
        self.sems = []
        self.pe = self._eng("pe")
        self.dve = self._eng("dve")
        self.act = self._eng("act")
        self.pool = self._eng("pool", dma=True)
        self.qin = self._eng("qin", dma=True)
        self._const_vals = {}
        self._const_tile = None
        self._n_const = 0
        self._n_psum = 0
        self.selfw_default = True

    def _eng(self, name, dma=False):
        s = Sem(name + "_sem")
        self.sems.append(s)
        pool = None
        if dma:
            pool = [Sem(f"{name}_s{i}") for i in range(8)]
            self.sems.extend(pool)
        return Eng(self, name, s, is_dma_queue=dma, sem_pool=pool)

    def sbuf(self, name, shape, dtype):
        t = self.stack.enter_context(self.nc.sbuf_tensor(name, list(shape), dtype))
        return t

    def psum(self, shape, dtype=F32, name=None):
        if name is None:
            name = f"ps{self._n_psum}"
            self._n_psum += 1
        return self.stack.enter_context(
            self.nc.psum_tensor(name, list(shape), dtype))

    def dram_in(self, name, shape, dtype):
        return self.nc.dram_tensor(name, list(shape), dtype, kind="ExternalInput").ap()

    def dram_out(self, name, shape, dtype):
        return self.nc.dram_tensor(name, list(shape), dtype, kind="ExternalOutput").ap()

    def const_cell(self, val, p=128):
        if self._const_tile is None:
            self._const_tile = self.sbuf("consts", [128, 32], F32)
            self.const_buf = Buf("consts")
        key = float(val)
        if key not in self._const_vals:
            i = self._n_const
            assert i < 32
            self._n_const += 1
            col = self._const_tile[:, i:i + 1]
            self.op(self.dve, lambda e, col=col, v=key: e.memset(col, v),
                    writes=[self.const_buf])
            self._const_vals[key] = i
        i = self._const_vals[key]
        return self._const_tile[0:p, i:i + 1]

    def op(self, eng, fn, reads=(), writes=(), inc=1, tag_reads=True, selfw=None):
        if selfw is None:
            selfw = self.selfw_default
        for b in reads:
            if b.writer:
                eng.wait(*b.writer, allow_self=selfw)
        for b in writes:
            if b.writer:
                eng.wait(*b.writer)
            for r in b.readers:
                eng.wait(*r)
        banks_read = []
        if eng is not self.pe:
            for b in reads:
                if getattr(b, "bank", None) is not None:
                    if b.bank.pe_tag:
                        eng.wait(*b.bank.pe_tag)
                    banks_read.append(b.bank)
        if inc:
            if eng.sem_pool is not None and inc == 16:
                sem = eng.sem_pool[eng.rr % len(eng.sem_pool)]
                eng.rr += 1
                if sem.n:
                    eng.wait(sem, sem.n)
                sem.n += 16
                tag = (sem, sem.n)
                eng.items.append(("op", (fn, sem, 16)))
            else:
                eng.sem.n += inc
                tag = (eng.sem, eng.sem.n)
                eng.items.append(("op", (fn, eng.sem, inc)))
        else:
            tag = None
            eng.items.append(("op", (fn, None, 0)))
        if tag is not None:
            for b in writes:
                b.writer = tag
                b.readers = []
            if tag_reads:
                for b in reads:
                    b.readers.append(tag)
            for bk in banks_read:
                bk.readers.append(tag)
        return tag

    def finalize_queues(self):
        for eng in (self.pool, self.qin):
            for s in (eng.sem_pool or []):
                if s.n:
                    eng.items.append(("wait", (s, s.n)))
            if eng.sem.n:
                eng.items.append(("wait", (eng.sem, eng.sem.n)))

    def mm_group(self, psum_buf, parts, inc=1):
        eng = self.pe
        pbufs = psum_buf if isinstance(psum_buf, (list, tuple)) else [psum_buf]
        allreads = []
        for fn, reads in parts:
            for b in reads:
                if b.writer:
                    eng.wait(*b.writer)
                allreads.append(b)
        banks = []
        for pb in pbufs:
            if pb.writer:
                eng.wait(*pb.writer)
            for r in pb.readers:
                eng.wait(*r)
            bk = getattr(pb, "bank", None)
            if bk is not None and bk not in banks:
                banks.append(bk)
        for bk in banks:
            for r in bk.readers:
                eng.wait(*r)
        for i, (fn, reads) in enumerate(parts):
            last = i == len(parts) - 1
            if last:
                eng.sem.n += inc
                eng.items.append(("op", (fn, eng.sem, inc)))
            else:
                eng.items.append(("op", (fn, None, 0)))
        tag = (eng.sem, eng.sem.n)
        for pb in pbufs:
            pb.writer = tag
            pb.readers = []
        for bk in banks:
            bk.pe_tag = tag
            bk.readers = []
        for b in allreads:
            b.readers.append(tag)
        return tag

    def build(self):
        nc = self.nc
        for s in self.sems:
            s.h = self.stack.enter_context(nc.semaphore(s.name))
        block = self.stack.enter_context(nc.Block())

        def replay(items):
            def run(raw):
                for kind, payload in items:
                    if kind == "wait":
                        sem, count = payload
                        raw.wait_ge(sem.h, count)
                    else:
                        fn, sem, inc = payload
                        instr = fn(raw)
                        if sem is not None:
                            instr.then_inc(sem.h, inc)
            return run

        block.tensor(replay(self.pe.items))
        block.vector(replay(self.dve.items))
        block.scalar(replay(self.act.items))
        block.gpsimd(replay(self.pool.items))
        block.sync(replay(self.qin.items))
        self.stack.close()
        return nc



def build_l0():
    c = Ctx("l0")
    c.selfw_default = False
    nc = c.nc

    pk_g = c.dram_in("pk_g", [128, MLP], U8)
    pk_u = c.dram_in("pk_u", [128, MLP], U8)
    pk_d = c.dram_in("pk_d", [1024, D], U8)
    sc_g = c.dram_in("sc_g", [8, MLP], F32)
    sc_u = c.dram_in("sc_u", [8, MLP], F32)
    sc_d = c.dram_in("sc_d", [64, D], F32)
    p8 = c.dram_in("p8", [8, 128], F32)
    w_in = {}
    for nm in ("wq", "wk", "wv", "wga", "wo"):
        w_in[nm] = c.dram_in(nm, [256, D], F32)
    ws1 = c.dram_in("ws1", [D, 16], F32)
    ws2 = c.dram_in("ws2", [16, 1024], F32)

    o_g = c.dram_out("o_g", [256, MLP], BF16)
    o_u = c.dram_out("o_u", [256, MLP], BF16)
    o_d = c.dram_out("o_d", [D, D], BF16)
    w_out = {}
    for nm in ("wq", "wk", "wv", "wga", "wo"):
        w_out[nm] = c.dram_out("o" + nm, [256, D], BF16)
    o_ws1 = c.dram_out("o_ws1", [D, 16], BF16)
    o_ws2 = c.dram_out("o_ws2", [16, 1024], BF16)

    W = 2048

    vi = c.sbuf("vi", [128, W], I32)
    l3 = c.sbuf("l3", [128, W], I32)
    h3 = c.sbuf("h3", [128, W], I32)
    bit = c.sbuf("bit", [128, W], I32)
    zl = c.sbuf("zl", [128, W], F32)
    zh = c.sbuf("zh", [128, W], F32)
    xf = c.sbuf("xf", [128, W], F32)
    r1 = c.sbuf("r1", [128, W], F32)
    r2 = c.sbuf("r2", [128, W], F32)
    mag = c.sbuf("mag", [128, W], F32)
    val = c.sbuf("val", [128, W], F32)
    wout = c.sbuf("wout", [128, 2, W], BF16)
    p8t = c.sbuf("p8t", [8, 128], F32)
    scb = c.sbuf("scb", [8, MLP], F32)
    scx = c.sbuf("scx", [128, W], F32)
    ps_sc = c.psum([128, 512], F32)

    bufs = {nm: Buf(nm) for nm in
            ("vi", "l3", "h3", "bit", "zl", "zh", "xf", "r1", "r2", "mag",
             "val", "wout", "p8t", "scb", "scx", "ps_sc", "wtile", "wbf")}
    bv = bufs

    c.op(c.qin, lambda e: e.dma_start(out=p8t[:], in_=p8[:]), writes=[bv["p8t"]], inc=16)

    def decode_tile(pk_ap, out_ap, sc_off, width, neg_half):
        c.op(c.pool, lambda e: e.dma_start(out=vi[:, :width], in_=pk_ap),
             writes=[bv["vi"]], inc=16)
        c.op(c.dve, lambda e: e.tensor_scalar(out=l3[:, :width], in0=vi[:, :width],
             scalar1=7, scalar2=None, op0=OP.bitwise_and),
             reads=[bv["vi"]], writes=[bv["l3"]])
        c.op(c.dve, lambda e: e.tensor_scalar(out=h3[:, :width], in0=vi[:, :width],
             scalar1=4, scalar2=7, op0=OP.logical_shift_right, op1=OP.bitwise_and),
             reads=[bv["vi"]], writes=[bv["h3"]])
        zsc = -1.0 if neg_half else -2.0
        zof = 0.5 if neg_half else 1.0
        c.op(c.dve, lambda e: e.tensor_scalar(out=bit[:, :width], in0=vi[:, :width],
             scalar1=3, scalar2=1, op0=OP.logical_shift_right, op1=OP.bitwise_and),
             reads=[bv["vi"]], writes=[bv["bit"]])
        c.op(c.dve, lambda e, zsc=zsc, zof=zof: e.tensor_scalar(
             out=zl[:, :width], in0=bit[:, :width],
             scalar1=zsc, scalar2=zof, op0=OP.mult, op1=OP.add),
             reads=[bv["bit"]], writes=[bv["zl"]])
        c.op(c.dve, lambda e: e.tensor_scalar(out=bit[:, :width], in0=vi[:, :width],
             scalar1=7, scalar2=1, op0=OP.logical_shift_right, op1=OP.bitwise_and),
             reads=[bv["vi"]], writes=[bv["bit"]])
        c.op(c.dve, lambda e, zsc=zsc, zof=zof: e.tensor_scalar(
             out=zh[:, :width], in0=bit[:, :width],
             scalar1=zsc, scalar2=zof, op0=OP.mult, op1=OP.add),
             reads=[bv["bit"]], writes=[bv["zh"]])
        for s0 in range(0, width, 512):
            sw = min(512, width - s0)
            c.mm_group(bv["ps_sc"], [
                (lambda e, s0=s0, sw=sw: e.matmul(ps_sc[:, :sw],
                    p8t[:], scb[:, sc_off + s0:sc_off + s0 + sw],
                    start=True, stop=True), [bv["p8t"], bv["scb"]]),
            ])
            c.op(c.dve, lambda e, s0=s0, sw=sw: e.tensor_copy(
                out=scx[:, s0:s0 + sw], in_=ps_sc[:, :sw]),
                reads=[bv["ps_sc"]], writes=[bv["scx"]])
        m4 = c.const_cell(-4.0)
        m6 = c.const_cell(-6.0)
        for parity in range(2):
            code = l3 if parity == 0 else h3
            cbuf = bv["l3" if parity == 0 else "h3"]
            zb = zl if parity == 0 else zh
            zbuf = bv["zl" if parity == 0 else "zh"]
            c.op(c.dve, lambda e, code=code: e.tensor_copy(
                out=xf[:, :width], in_=code[:, :width]),
                reads=[cbuf], writes=[bv["xf"]])
            c.op(c.act, lambda e: e.activation(out=r1[:, :width], in_=xf[:, :width],
                 func=AF.Relu, scale=1.0, bias=m4),
                 reads=[bv["xf"], c.const_buf], writes=[bv["r1"]])
            c.op(c.act, lambda e: e.activation(out=r2[:, :width], in_=xf[:, :width],
                 func=AF.Relu, scale=1.0, bias=m6),
                 reads=[bv["xf"], c.const_buf], writes=[bv["r2"]])
            c.op(c.dve, lambda e: e.scalar_tensor_tensor(out=mag[:, :width],
                 in0=r1[:, :width], scalar=0.5, in1=r2[:, :width],
                 op0=OP.mult, op1=OP.add),
                 reads=[bv["r1"], bv["r2"]], writes=[bv["mag"]])
            c.op(c.dve, lambda e: e.scalar_tensor_tensor(out=mag[:, :width],
                 in0=xf[:, :width], scalar=0.5, in1=mag[:, :width],
                 op0=OP.mult, op1=OP.add),
                 reads=[bv["xf"], bv["mag"]], writes=[bv["mag"]])
            c.op(c.dve, lambda e, zb=zb: e.tensor_tensor(out=val[:, :width],
                 in0=zb[:, :width], in1=mag[:, :width], op=OP.mult),
                 reads=[zbuf, bv["mag"]], writes=[bv["val"]])
            c.op(c.dve, lambda e, parity=parity: e.tensor_tensor(
                 out=wout[:, parity, :width], in0=val[:, :width],
                 in1=scx[:, :width], op=OP.mult),
                 reads=[bv["val"], bv["scx"]], writes=[bv["wout"]])
        c.op(c.pool, lambda e: e.dma_start(out=out_ap, in_=wout[:, :, :width]),
             reads=[bv["wout"]], inc=16)

    for (pk, scin, outt) in ((pk_g, sc_g, o_g), (pk_u, sc_u, o_u)):
        c.op(c.qin, lambda e, scin=scin: e.dma_start(out=scb[:], in_=scin[:]),
             writes=[bv["scb"]], inc=16)
        ov = outt.rearrange("(kb two) n -> kb two n", two=2)
        for n0 in range(0, MLP, W):
            decode_tile(pk[:, n0:n0 + W], ov[:, :, n0:n0 + W], n0, W,
                        neg_half=False)
    c.op(c.qin, lambda e: e.dma_start(
        out=scb[:, :8 * D].rearrange("p (t n) -> p t n", n=D),
        in_=sc_d.rearrange("(t p) n -> p t n", p=8)),
        writes=[bv["scb"]], inc=16)
    od = o_d.rearrange("(t kb two) n -> t kb two n", t=8, two=2)
    for t in range(8):
        decode_tile(pk_d[128 * t:128 * (t + 1), :], od[t], t * D, D,
                    neg_half=True)

    wtile = c.sbuf("wtile", [128, D], F32)
    wbf = c.sbuf("wbf", [128, D], BF16)
    for nm in ("wq", "wk", "wv", "wga", "wo"):
        for r in range(2):
            c.op(c.qin, lambda e, nm=nm, r=r: e.dma_start(
                out=wtile[:], in_=w_in[nm][128 * r:128 * (r + 1), :]),
                writes=[bv["wtile"]], inc=16)
            c.op(c.dve, lambda e: e.tensor_copy(out=wbf[:], in_=wtile[:]),
                 reads=[bv["wtile"]], writes=[bv["wbf"]])
            c.op(c.pool, lambda e, nm=nm, r=r: e.dma_start(
                out=w_out[nm][128 * r:128 * (r + 1), :], in_=wbf[:]),
                reads=[bv["wbf"]], inc=16)
    ws1v = ws1.rearrange("(t p) k -> p t k", p=128)
    ows1v = o_ws1.rearrange("(t p) k -> p t k", p=128)
    c.op(c.qin, lambda e: e.dma_start(
        out=wtile[:, :256].rearrange("p (t k) -> p t k", k=16), in_=ws1v),
        writes=[bv["wtile"]], inc=16)
    c.op(c.dve, lambda e: e.tensor_copy(out=wbf[:, :256], in_=wtile[:, :256]),
         reads=[bv["wtile"]], writes=[bv["wbf"]])
    c.op(c.pool, lambda e: e.dma_start(
        out=ows1v, in_=wbf[:, :256].rearrange("p (t k) -> p t k", k=16)),
        reads=[bv["wbf"]], inc=16)
    c.op(c.qin, lambda e: e.dma_start(out=wtile[:16, :1024], in_=ws2[:]),
         writes=[bv["wtile"]], inc=16)
    c.op(c.dve, lambda e: e.tensor_copy(out=wbf[:16, :1024], in_=wtile[:16, :1024]),
         reads=[bv["wtile"]], writes=[bv["wbf"]])
    c.op(c.pool, lambda e: e.dma_start(out=o_ws2[:], in_=wbf[:16, :1024]),
         reads=[bv["wbf"]], inc=16)

    c.finalize_queues()
    return c.build()



def _emit_ln(c, sb, bv, pss, tag):
    x, xln = sb["x"], sb["xln"]
    sq0, sq1 = sb["sq0"], sb["sq1"]
    stat = sb["stat"]
    ones_col, ones_row = sb["ones_col"], sb["ones_row"]
    ps_b, ps_b2 = pss["ps_b"], pss["ps_b2"]
    bx, bxln = bv["x"], bv["xln"]
    bsq = (bv["sq0"], bv["sq1"])
    bstat, bones = bv["stat"], bv["ones"]
    bpb, bpb2 = bv["ps_b"], bv["ps_b2"]
    eps = c.const_cell(EPS, p=1)
    for tt in range(2):
        t0 = 512 * tt
        c.mm_group(bpb, [
            (lambda e, dc=dc, t0=t0: e.matmul(
                ps_b[0:1, :], ones_col[:], x[:, dc, t0:t0 + 512],
                start=(dc == 0), stop=(dc == 15)), [bx, bones])
            for dc in range(16)])
        c.op(c.dve, lambda e, t0=t0: e.tensor_scalar(
            out=stat[0:1, 0, :], in0=ps_b[0:1, :], scalar1=1.0 / D,
            scalar2=None, op0=OP.mult),
            reads=[bpb], writes=[bstat])
        for dc in range(16):
            sq = sq0 if dc % 2 == 0 else sq1
            c.op(c.act, lambda e, dc=dc, t0=t0, sq=sq: e.activation(
                out=sq[:], in_=x[:, dc, t0:t0 + 512], func=AF.Square),
                reads=[bx], writes=[bsq[dc % 2]])
            c.mm_group(bpb2, [
                (lambda e, dc=dc, sq=sq: e.matmul(
                    ps_b2[0:1, :], ones_col[:], sq[:],
                    start=(dc == 0), stop=(dc == 15)), [bsq[dc % 2], bones])
            ], inc=1)
        c.op(c.dve, lambda e: e.tensor_scalar(
            out=stat[0:1, 1, :], in0=ps_b2[0:1, :], scalar1=1.0 / D,
            scalar2=None, op0=OP.mult),
            reads=[bpb2], writes=[bstat])
        c.op(c.dve, lambda e: e.tensor_tensor(
            out=stat[0:1, 2, :], in0=stat[0:1, 0, :], in1=stat[0:1, 0, :],
            op=OP.mult), reads=[bstat], writes=[bstat])
        c.op(c.dve, lambda e: e.tensor_tensor(
            out=stat[0:1, 1, :], in0=stat[0:1, 1, :], in1=stat[0:1, 2, :],
            op=OP.subtract), reads=[bstat], writes=[bstat])
        c.op(c.act, lambda e: e.activation(
            out=stat[0:1, 2, :], in_=stat[0:1, 1, :], func=AF.Ln,
            scale=1.0, bias=eps), reads=[bstat, c.const_buf], writes=[bstat])
        c.op(c.act, lambda e: e.activation(
            out=stat[0:1, 3, :], in_=stat[0:1, 2, :], func=AF.Exp,
            scale=-0.5), reads=[bstat], writes=[bstat])
        c.op(c.dve, lambda e: e.tensor_tensor(
            out=stat[0:1, 4, :], in0=stat[0:1, 0, :], in1=stat[0:1, 3, :],
            op=OP.mult), reads=[bstat], writes=[bstat])
        c.mm_group(bpb, [(lambda e: e.matmul(
            ps_b[:, :], ones_row[:], stat[0:1, 3, :], start=True, stop=True),
            [bstat, bones])])
        c.mm_group(bpb2, [(lambda e: e.matmul(
            ps_b2[:, :], ones_row[:], stat[0:1, 4, :], start=True, stop=True),
            [bstat, bones])])
        for dc in range(16):
            sq = sq0 if dc % 2 == 0 else sq1
            c.op(c.dve, lambda e, dc=dc, t0=t0, sq=sq: e.tensor_tensor(
                out=sq[:], in0=x[:, dc, t0:t0 + 512], in1=ps_b[:, :],
                op=OP.mult), reads=[bx, bpb], writes=[bsq[dc % 2]])
            c.op(c.dve, lambda e, dc=dc, t0=t0, sq=sq: e.tensor_tensor(
                out=xln[:, dc, t0:t0 + 512], in0=sq[:], in1=ps_b2[:, :],
                op=OP.subtract), reads=[bsq[dc % 2], bpb2], writes=[bxln])


def _emit_ffn(c, sb, bv, pss, wgt, wut, wdt):
    x, xln = sb["x"], sb["xln"]
    wa, gate_g, inter = sb["wa"], sb["gate_g"], sb["inter"]
    bx, bxln = bv["x"], bv["xln"]
    bwa = (bv["wa0"], bv["wa1"])
    bgg, bint = bv["gate_g"], bv["inter"]
    ps_g = (pss["ps_g0"], pss["ps_g1"])
    bps_g = (bv["ps_g0"], bv["ps_g1"])
    ps_u = (pss["ps_u0"], pss["ps_u1"])
    bps_u = (bv["ps_u0"], bv["ps_u1"])
    ps_d = (pss["ps_d0"], pss["ps_d1"])
    bps_d = (bv["ps_d0"], bv["ps_d1"])
    wgv = wgt.rearrange("(kc p) n -> p kc n", p=128)
    wuv = wut.rearrange("(kc p) n -> p kc n", p=128)
    wdv = wdt.rearrange("(nc p) d -> p nc d", p=128)

    slot = [0]

    def load(view, cols, nrows=16, row0=0):
        s = slot[0]
        slot[0] = 1 - s
        c.op(c.qin, lambda e: e.dma_start(
            out=wa[:, s, 0:nrows, :], in_=view[:, row0:row0 + nrows, cols]),
            writes=[bwa[s]], inc=16)
        return s

    for g in range(16):
        for w2 in range(2):
            n0 = 1024 * g + 512 * w2
            sg = load(wgv, slice(n0, n0 + 512))
            su = load(wuv, slice(n0, n0 + 512))
            for nc4 in range(4):
                ic = 4 * w2 + nc4
                for tt in range(2):
                    t0 = 512 * tt
                    i = tt
                    c.mm_group(bps_g[i], [
                        (lambda e, kc=kc, nc4=nc4, t0=t0, i=i, sg=sg: e.matmul(
                            ps_g[i][:], wa[:, sg, kc, 128 * nc4:128 * (nc4 + 1)],
                            xln[:, kc, t0:t0 + 512],
                            start=(kc == 0), stop=(kc == 15)),
                         [bwa[sg], bxln]) for kc in range(16)])
                    c.op(c.act, lambda e, nc4=nc4, t0=t0, i=i: e.activation(
                        out=gate_g[:, nc4, t0:t0 + 512], in_=ps_g[i][:],
                        func=AF.Gelu_apprx_tanh),
                        reads=[bps_g[i]], writes=[bgg])
                for tt in range(2):
                    t0 = 512 * tt
                    i = tt
                    c.mm_group(bps_u[i], [
                        (lambda e, kc=kc, nc4=nc4, t0=t0, i=i, su=su: e.matmul(
                            ps_u[i][:], wa[:, su, kc, 128 * nc4:128 * (nc4 + 1)],
                            xln[:, kc, t0:t0 + 512],
                            start=(kc == 0), stop=(kc == 15)),
                         [bwa[su], bxln]) for kc in range(16)])
                    c.op(c.dve, lambda e, ic=ic, nc4=nc4, t0=t0, i=i: e.tensor_tensor(
                        out=inter[:, ic, t0:t0 + 512], in0=gate_g[:, nc4, t0:t0 + 512],
                        in1=ps_u[i][:], op=OP.mult),
                        reads=[bgg, bps_u[i]], writes=[bint])
        for dwin in range(4):
            sd = load(wdv, slice(512 * dwin, 512 * (dwin + 1)), nrows=8, row0=8 * g)
            for dc4 in range(4):
                dchunk = 4 * dwin + dc4
                for tt in range(2):
                    t0 = 512 * tt
                    i = tt
                    c.mm_group(bps_d[i], [
                        (lambda e, ncj=ncj, dc4=dc4, t0=t0, i=i, sd=sd: e.matmul(
                            ps_d[i][:], wa[:, sd, ncj, 128 * dc4:128 * (dc4 + 1)],
                            inter[:, ncj, t0:t0 + 512],
                            start=(ncj == 0), stop=(ncj == 7)),
                         [bwa[sd], bint]) for ncj in range(8)])
                    c.op(c.dve, lambda e, dchunk=dchunk, t0=t0, i=i: e.tensor_tensor(
                        out=x[:, dchunk, t0:t0 + 512], in0=x[:, dchunk, t0:t0 + 512],
                        in1=ps_d[i][:], op=OP.add),
                        reads=[bps_d[i], bx], writes=[bx])


def build_ffn_launch(is_l3):
    c = Ctx("l3" if is_l3 else "l1")
    xin = c.dram_in("xin", [D, TC], F32)
    wgt = c.dram_in("wgt", [D, MLP], BF16)
    wut = c.dram_in("wut", [D, MLP], BF16)
    wdt = c.dram_in("wdt", [MLP, D], BF16)
    if is_l3:
        og_in = c.dram_in("og", [D, TC], BF16)
        wot_in = c.dram_in("wot", [D, D], BF16)
        xout = c.dram_out("xout", [D, TC], F32)
    else:
        xout = c.dram_out("xout", [D, TC], F32)
        xlnout = c.dram_out("xlnout", [D, TC], BF16)

    sb = {}
    sb["x"] = c.sbuf("x", [128, 16, TC], F32)
    sb["xln"] = c.sbuf("xln", [128, 16, TC], BF16)
    sb["wa"] = c.sbuf("wa", [128, 2, 16, 512], BF16)
    sb["gate_g"] = c.sbuf("gate_g", [128, 4, TC], BF16)
    sb["inter"] = c.sbuf("inter", [128, 8, TC], BF16)
    sb["sq0"] = c.sbuf("sq0", [128, 512], F32)
    sb["sq1"] = c.sbuf("sq1", [128, 512], F32)
    sb["stat"] = c.sbuf("stat", [1, 8, 512], F32)
    sb["ones_col"] = c.sbuf("ones_col", [128, 1], F32)
    sb["ones_row"] = c.sbuf("ones_row", [1, 128], F32)
    if is_l3:
        sb["og"] = c.sbuf("ogsb", [128, 16, TC], BF16)

    pss = {}
    for nm, shp in (("ps_b", [128, 512]), ("ps_b2", [128, 512]),
                    ("ps_g0", [128, 512]), ("ps_g1", [128, 512]),
                    ("ps_u0", [128, 512]), ("ps_u1", [128, 512]),
                    ("ps_d0", [128, 512]), ("ps_d1", [128, 512])):
        pss[nm] = c.psum(shp, F32, name=nm)

    bv = {nm: Buf(nm) for nm in
          ("x", "xln", "wa0", "wa1", "gate_g", "inter", "sq0", "sq1",
           "stat", "ones", "og",
           "ps_b", "ps_b2", "ps_g0", "ps_g1", "ps_u0", "ps_u1",
           "ps_d0", "ps_d1")}

    c.op(c.dve, lambda e: e.memset(sb["ones_col"][:], 1.0), writes=[bv["ones"]])
    c.op(c.dve, lambda e: e.memset(sb["ones_row"][:], 1.0), writes=[bv["ones"]])
    c.op(c.qin, lambda e: e.dma_start(
        out=sb["x"][:], in_=xin.rearrange("(dc p) t -> p dc t", p=128)),
        writes=[bv["x"]], inc=16)

    if is_l3:
        c.op(c.qin, lambda e: e.dma_start(
            out=sb["og"][:], in_=og_in.rearrange("(hc p) t -> p hc t", p=128)),
            writes=[bv["og"]], inc=16)
        wov = wot_in.rearrange("(hc p) d -> p hc d", p=128)
        wslot = [0]
        for dwin in range(4):
            s = wslot[0]
            wslot[0] = 1 - s
            c.op(c.qin, lambda e, dwin=dwin, s=s: e.dma_start(
                out=sb["wa"][:, s, :, :],
                in_=wov[:, :, 512 * dwin:512 * (dwin + 1)]),
                writes=[bv["wa%d" % s]], inc=16)
            for dc4 in range(4):
                dchunk = 4 * dwin + dc4
                for tt in range(2):
                    t0 = 512 * tt
                    i = (dchunk * 2 + tt) % 2
                    psd = pss["ps_d%d" % i]
                    c.mm_group(bv["ps_d%d" % i], [
                        (lambda e, hc=hc, dc4=dc4, t0=t0, psd=psd, s=s: e.matmul(
                            psd[:], sb["wa"][:, s, hc, 128 * dc4:128 * (dc4 + 1)],
                            sb["og"][:, hc, t0:t0 + 512],
                            start=(hc == 0), stop=(hc == 15)),
                         [bv["wa%d" % s], bv["og"]]) for hc in range(16)])
                    c.op(c.dve, lambda e, dchunk=dchunk, t0=t0, i=i: e.tensor_tensor(
                        out=sb["x"][:, dchunk, t0:t0 + 512],
                        in0=sb["x"][:, dchunk, t0:t0 + 512],
                        in1=pss["ps_d%d" % i][:], op=OP.add),
                        reads=[bv["ps_d%d" % i], bv["x"]], writes=[bv["x"]])

    _emit_ln(c, sb, bv, pss, "ln")
    _emit_ffn(c, sb, bv, pss, wgt, wut, wdt)

    if not is_l3:
        _emit_ln(c, sb, bv, pss, "ln2")
        c.op(c.pool, lambda e: e.dma_start(
            out=xlnout.rearrange("(dc p) t -> p dc t", p=128), in_=sb["xln"][:]),
            reads=[bv["xln"]], inc=16)
    c.op(c.pool, lambda e: e.dma_start(
        out=xout.rearrange("(dc p) t -> p dc t", p=128), in_=sb["x"][:]),
        reads=[bv["x"]], inc=16)
    c.finalize_queues()
    return c.build()



def build_l2(debug=False, phases="all", nsu=4, stmask="012345", ilv=True, ilv_n=99):
    assert not debug, "debug hooks removed in the interleaved version"
    c = Ctx("l2")
    xall = c.dram_in("xall", [D, B * T], BF16)
    wq = c.dram_in("wq", [D, 256], BF16)
    wk = c.dram_in("wk", [D, 256], BF16)
    wv = c.dram_in("wv", [D, 256], BF16)
    wga = c.dram_in("wga", [D, 256], BF16)
    ws1 = c.dram_in("ws1", [D, 16], BF16)
    ws2 = c.dram_in("ws2", [16, 128], BF16)
    bs2A = c.dram_in("bs2A", [64, 1], F32)
    bs2B = c.dram_in("bs2B", [64, 1], F32)
    bs2r = c.dram_in("bs2r", [1, 128], F32)
    gnwr = c.dram_in("gnwr", [1, 128], F32)
    cosd = c.dram_in("cosd", [128, T], BF16)
    sind = c.dram_in("sind", [128, T], BF16)
    triu = c.dram_in("triu", [128, 128], BF16)
    idn = c.dram_in("idn", [128, 128], BF16)
    ogo = c.dram_out("ogo", [256, B * T], BF16)

    xw = c.sbuf("xw", [128, 16, 512], BF16)
    wpr = c.sbuf("wpr", [128, 4, 16, 256], BF16)
    ws1s = c.sbuf("ws1s", [128, 16, 16], BF16)
    ws2s = c.sbuf("ws2s", [16, 128], BF16)
    bs2rs = c.sbuf("bs2rs", [1, 128], BF16)
    qkvs = [c.sbuf("qkvs0", [128, 4, 2048 * 2], BF16),
            c.sbuf("qkvs1", [128, 4, 2048 * 2], BF16)]
    rpb1 = c.sbuf("rpb1", [128, 2048], BF16)
    rpb2 = c.sbuf("rpb2", [128, 2048], BF16)
    s1a = [c.sbuf("s1a0", [16, 2048], BF16), c.sbuf("s1a1", [16, 2048], BF16)]
    esmm = [[c.sbuf("esmm0A", [64, 2048], BF16), c.sbuf("esmm0B", [64, 2048], BF16)],
            [c.sbuf("esmm1A", [64, 2048], BF16), c.sbuf("esmm1B", [64, 2048], BF16)]]
    estm = [c.sbuf("estm0", [128, 16, 128], BF16),
            c.sbuf("estm1", [128, 16, 128], BF16)]
    csb = c.sbuf("csb", [128, T], BF16)
    snb = c.sbuf("snb", [128, T], BF16)
    trs = c.sbuf("trs", [128, 128], BF16)
    ids = c.sbuf("ids", [128, 128], BF16)
    bsA = c.sbuf("bsA", [64, 1], F32)
    bsB = c.sbuf("bsB", [64, 1], F32)
    ones_col = c.sbuf("onc", [128, 1], F32)
    ones_row = c.sbuf("onr", [1, 128], F32)
    onesb_row = c.sbuf("onbr", [1, 128], BF16)
    onesb_col = c.sbuf("onbc", [128, 1], BF16)
    zcol = c.sbuf("zcol", [1, 128], BF16)
    zrow = c.sbuf("zrow", [1, 512], BF16)
    ogsb = c.sbuf("ogsb2", [128, 2, 2048], BF16)
    kts = [c.sbuf("ktsA", [128, 128], BF16), c.sbuf("ktsB", [128, 128], BF16)]
    vts = [c.sbuf("vtsA", [128, 128], BF16), c.sbuf("vtsB", [128, 128], BF16)]
    ats = [c.sbuf("atsA", [128, 128], BF16), c.sbuf("atsB", [128, 128], BF16)]
    w2s = [c.sbuf("w2sA", [128, 128], BF16), c.sbuf("w2sB", [128, 128], BF16)]
    qeps = [c.sbuf("qepsA", [128, 128], BF16), c.sbuf("qepsB", [128, 128], BF16)]
    qets = [c.sbuf("qetsA", [64, 128], BF16), c.sbuf("qetsB", [64, 128], BF16)]
    enzc = c.sbuf("enzc", [128, 128], F32)
    okf = [c.sbuf("okfA", [128, 64], F32), c.sbuf("okfB", [128, 64], F32)]
    exf = [c.sbuf("exfA", [128, 64], F32), c.sbuf("exfB", [128, 64], F32)]
    mxs = [c.sbuf("mxsA", [128, 2], F32), c.sbuf("mxsB", [128, 2], F32)]
    s1sb = [c.sbuf("s1sbA", [128, 64], BF16), c.sbuf("s1sbB", [128, 64], BF16)]
    s2sb = [c.sbuf("s2sbA", [64, 128], BF16), c.sbuf("s2sbB", [64, 128], BF16)]
    c1f = c.sbuf("c1f", [1, 128], F32)
    c1b = c.sbuf("c1b", [1, 128], BF16)
    oraw = [c.sbuf("orawA", [128, 2048], BF16), c.sbuf("orawB", [128, 2048], BF16)]
    rnb = [c.sbuf("rnbA", [1, 2048], BF16), c.sbuf("rnbB", [1, 2048], BF16)]
    rbw = c.sbuf("rbw", [128, 512], F32)
    onw = c.sbuf("onw", [128, 512], F32)
    gnwrb = c.sbuf("gnwrb", [1, 128], BF16)

    ps_states = c.psum([128, 512], F32, name="ps_states")
    ps_f1 = [c.psum([128, 512], F32, name="ps_f1a"),
             c.psum([128, 512], F32, name="ps_f1b")]
    ps_f2 = [c.psum([128, 512], F32, name="ps_f2a"),
             c.psum([128, 512], F32, name="ps_f2b")]
    ps_bf = c.psum([128, 1024], BF16, name="ps_bf")
    ps_in = [c.psum([128, 512], F32, name="ps_in0"),
             c.psum([128, 512], F32, name="ps_in1")]

    bv = {}
    for nm in ("xw", "wpr", "ws1s", "ws2s", "bs2rs", "rpb1",
               "rpb2", "rope", "trs", "ids", "bsA", "bsB",
               "ones", "ogsb", "ps_states", "rbw", "onw", "gnws"):
        bv[nm] = Buf(nm)
    for nm in ("qkvs", "s1a", "estm",
               "kts", "vts", "ats", "w2s", "qeps", "qets",
               "okf", "exf", "mxs", "s1sb", "s2sb",
               "oraw", "rnb",
               "ps_f1", "ps_f2", "ps_bf",
               "r_at", "r_w2", "r_o", "r_ok", "r_osum",
               "r_rb", "r_kt", "r_vt", "r_qet"):
        bv[nm] = [Buf(nm + "0"), Buf(nm + "1")]
    for nm in ("enzc", "c1f", "c1b", "r_cum", "r_col"):
        bv[nm] = Buf(nm)
    bv["esmm"] = [[Buf("esmm0A"), Buf("esmm0B")], [Buf("esmm1A"), Buf("esmm1B")]]
    bv["ps_in"] = [Buf("ps_in0"), Buf("ps_in1")]
    bank_f1 = [Bank(), Bank()]
    bank_f2 = [Bank(), Bank()]
    bank_bf = Bank()
    bank_in = [Bank(), Bank()]
    bank_states = Bank()
    bv["ps_states"].bank = bank_states
    bv["ps_in"][0].bank = bank_in[0]
    bv["ps_in"][1].bank = bank_in[1]
    for hb in range(2):
        for nm in ("ps_f1", "r_at", "r_w2", "r_o"):
            bv[nm][hb].bank = bank_f1[hb]
        for nm in ("ps_f2", "r_ok", "r_osum", "r_rb"):
            bv[nm][hb].bank = bank_f2[hb]
        for nm in ("ps_bf", "r_kt", "r_vt", "r_qet"):
            bv[nm][hb].bank = bank_bf
    bv["r_cum"].bank = bank_f2[0]
    bv["r_col"].bank = bank_f2[0]
    bv["r_ok"][0].bank = bank_f2[0]
    bv["r_ok"][1].bank = bank_f2[1]

    def f1_all(hb):
        return [bv["ps_f1"][hb], bv["r_at"][hb], bv["r_w2"][hb], bv["r_o"][hb]]

    def f2_all(hb):
        out = [bv["ps_f2"][hb], bv["r_ok"][hb], bv["r_osum"][hb], bv["r_rb"][hb]]
        if hb == 0:
            out += [bv["r_cum"], bv["r_col"]]
        return out

    c.op(c.dve, lambda e: e.memset(ones_col[:], 1.0), writes=[bv["ones"]])
    c.op(c.dve, lambda e: e.memset(ones_row[:], 1.0), writes=[bv["ones"]])
    c.op(c.dve, lambda e: e.memset(onesb_row[:], 1.0), writes=[bv["ones"]])
    c.op(c.dve, lambda e: e.memset(onesb_col[:], 1.0), writes=[bv["ones"]])
    c.op(c.dve, lambda e: e.memset(zcol[:], 0.0), writes=[bv["ones"]])
    c.op(c.dve, lambda e: e.memset(zrow[:], 0.0), writes=[bv["ones"]])
    for hb in range(2):
        c.op(c.dve, lambda e, hb=hb: e.memset(qeps[hb][:], 0.0),
             writes=[bv["qeps"][hb]])
    for nm, src, dst in (("trs", triu, trs), ("ids", idn, ids),
                         ("rope", cosd, csb), ("rope", sind, snb),
                         ("ws2s", ws2, ws2s)):
        c.op(c.qin, lambda e, src=src, dst=dst: e.dma_start(out=dst[:], in_=src[:]),
             writes=[bv[nm]], inc=16)
    c.op(c.pool, lambda e: e.dma_start(out=bs2rs[:], in_=bs2r[:]),
         writes=[bv["bs2rs"]], inc=16)
    c.op(c.pool, lambda e: e.dma_start(out=gnwrb[:], in_=gnwr[:]),
         writes=[bv["gnws"]], inc=16)
    c.op(c.qin, lambda e: e.dma_start(out=bsA[:], in_=bs2A[:]), writes=[bv["bsA"]], inc=16)
    c.op(c.qin, lambda e: e.dma_start(out=bsB[:], in_=bs2B[:]), writes=[bv["bsB"]], inc=16)
    for i, w in enumerate((wq, wk, wv, wga)):
        c.op(c.qin, lambda e, i=i, w=w: e.dma_start(
            out=wpr[:, i, :, :], in_=w.rearrange("(kc p) h -> p kc h", p=128)),
            writes=[bv["wpr"]], inc=16)
    c.op(c.qin, lambda e: e.dma_start(
        out=ws1s[:], in_=ws1.rearrange("(kc p) r -> p kc r", p=128)),
        writes=[bv["ws1s"]], inc=16)

    K_Q, K_K, K_V, K_S = 0, 1, 2, 3
    xv = xall.rearrange("(kc p) t -> p kc t", p=128)
    eps_c = c.const_cell(EPS, p=1)

    def hslice(head, sl):
        return slice(2048 * head + sl.start, 2048 * head + sl.stop)

    def unit_items(b, h2, bs):
        items = []
        tbase = b * T + h2 * 2048
        full = slice(0, 2048)

        for tw in range(4):
            t0 = tbase + 512 * tw
            dst = slice(512 * tw, 512 * (tw + 1))

            def load_xw(t0=t0):
                c.op(c.qin, lambda e: e.dma_start(
                    out=xw[:], in_=xv[:, :, t0:t0 + 512]),
                    writes=[bv["xw"]], inc=16)
            items.append(load_xw)
            for head in range(2):
                for pj in (K_Q, K_K, K_V, K_S):
                    def proj(head=head, pj=pj, dst=dst, bs=bs):
                        i = pj % 2
                        psb = ps_in[i]
                        pbv = [bv["ps_in"][i]]
                        hoff = 128 * head
                        c.mm_group(pbv, [
                            (lambda e, kc=kc, pj=pj, hoff=hoff, psb=psb: e.matmul(
                                psb[:], wpr[:, pj, kc, hoff:hoff + 128],
                                xw[:, kc, :], start=(kc == 0), stop=(kc == 15),
                                skip_group_check=True),
                             [bv["wpr"], bv["xw"]]) for kc in range(16)])
                        fn = AF.Silu if pj == K_S else AF.Identity
                        c.op(c.act, lambda e, fn=fn: e.activation(
                            out=qkvs[bs][:, pj, hslice(head, dst)],
                            in_=psb[:], func=fn),
                            reads=pbv, writes=[bv["qkvs"][bs]])
                    items.append(proj)

            def s1proj(dst=dst, bs=bs):
                c.mm_group([bv["ps_in"][0]], [
                    (lambda e, kc=kc: e.matmul(
                        ps_in[0][0:16, 0:512], ws1s[:, kc, :], xw[:, kc, :],
                        start=(kc == 0), stop=(kc == 15), skip_group_check=True),
                     [bv["ws1s"], bv["xw"]]) for kc in range(16)])
                c.op(c.dve, lambda e: e.tensor_copy(
                    out=s1a[bs][:, dst], in_=ps_in[0][0:16, 0:512]),
                    reads=[bv["ps_in"][0]], writes=[bv["s1a"][bs]])
            items.append(s1proj)

        rope_pos = slice(h2 * 2048, h2 * 2048 + 2048)
        for head in range(2):
            for pj in (K_Q, K_K):
                def rope(head=head, pj=pj, bs=bs, rope_pos=rope_pos):
                    src = qkvs[bs][:, pj, hslice(head, slice(0, 2048))]
                    c.op(c.pool, lambda e: e.dma_start(
                        out=rpb2[0:64, :], in_=src[64:128, :]),
                        reads=[bv["qkvs"][bs]], writes=[bv["rpb2"]], inc=16)
                    c.op(c.pool, lambda e: e.dma_start(
                        out=rpb2[64:128, :], in_=src[0:64, :]),
                        reads=[bv["qkvs"][bs]], writes=[bv["rpb2"]], inc=16)
                    c.op(c.dve, lambda e: e.tensor_tensor(
                        out=rpb1[:], in0=src, in1=csb[:, rope_pos], op=OP.mult),
                        reads=[bv["qkvs"][bs], bv["rope"]], writes=[bv["rpb1"]])
                    c.op(c.dve, lambda e: e.tensor_tensor(
                        out=rpb2[:], in0=rpb2[:], in1=snb[:, rope_pos], op=OP.mult),
                        reads=[bv["rpb2"], bv["rope"]], writes=[bv["rpb2"]])
                    c.op(c.dve, lambda e: e.tensor_tensor(
                        out=src, in0=rpb1[:], in1=rpb2[:], op=OP.subtract),
                        reads=[bv["rpb1"], bv["rpb2"]], writes=[bv["qkvs"][bs]])
                    if pj == K_Q:
                        c.op(c.dve, lambda e: e.tensor_scalar(
                            out=src, in0=src, scalar1=SCALE, scalar2=None,
                            op0=OP.mult),
                            reads=[bv["qkvs"][bs]], writes=[bv["qkvs"][bs]])
                items.append(rope)

        for head in range(2):
            for tw in range(4):
                def esm(head=head, tw=tw, bs=bs):
                    bsx = bsA if head == 0 else bsB
                    bsxb = bv["bsA"] if head == 0 else bv["bsB"]
                    dst = slice(512 * tw, 512 * (tw + 1))
                    c.mm_group([bv["ps_in"][head]], [(lambda e: e.matmul(
                        ps_in[head][0:64, 0:512],
                        ws2s[:, 64 * head:64 * (head + 1)],
                        s1a[bs][:, dst],
                        start=True, stop=True, skip_group_check=True),
                        [bv["ws2s"], bv["s1a"][bs]])])
                    c.op(c.dve, lambda e: e.tensor_scalar(
                        out=onw[0:64, 0:512], in0=ps_in[head][0:64, 0:512],
                        scalar1=bsx[:], scalar2=-CLAMP, op0=OP.add, op1=OP.max),
                        reads=[bv["ps_in"][head], bsxb], writes=[bv["onw"]])
                    c.op(c.dve, lambda e: e.tensor_scalar(
                        out=onw[0:64, 0:512], in0=onw[0:64, 0:512],
                        scalar1=CLAMP, scalar2=None, op0=OP.min),
                        reads=[bv["onw"]], writes=[bv["onw"]])
                    c.op(c.act, lambda e: e.activation(
                        out=esmm[bs][head][:, dst], in_=onw[0:64, 0:512],
                        func=AF.Exp),
                        reads=[bv["onw"]], writes=[bv["esmm"][bs][head]])
                items.append(esm)
        for tc in range(16):
            def est(tc=tc, bs=bs):
                i = tc % 2
                c.mm_group([bv["ps_in"][i]], [
                    (lambda e: e.matmul(
                        ps_in[i][:, 0:128], s1a[bs][:, 128 * tc:128 * (tc + 1)],
                        ws2s[:], start=True, stop=False, skip_group_check=True),
                     [bv["s1a"][bs], bv["ws2s"]]),
                    (lambda e: e.matmul(
                        ps_in[i][:, 0:128], onesb_row[:], bs2rs[:],
                        start=False, stop=True, skip_group_check=True),
                     [bv["ones"], bv["bs2rs"]]),
                ])
                c.op(c.dve, lambda e, i=i: e.tensor_scalar(
                    out=rbw[:, 0:128], in0=ps_in[i][:, 0:128],
                    scalar1=-CLAMP, scalar2=CLAMP, op0=OP.max, op1=OP.min),
                    reads=[bv["ps_in"][i]], writes=[bv["rbw"]])
                c.op(c.act, lambda e: e.activation(
                    out=estm[bs][:, tc, :], in_=rbw[:, 0:128], func=AF.Exp),
                    reads=[bv["rbw"]], writes=[bv["estm"][bs]])
            items.append(est)
        return items

    def chunk_stages(head, tc, first, last, bs):
        hb = head
        s1col = slice(64 * head, 64 * (head + 1))
        s2col = slice(128 + 128 * head, 256 + 128 * head)
        t128 = slice(128 * tc, 128 * (tc + 1))
        es_m = slice(64 * head, 64 * (head + 1))
        q_ap = qkvs[bs][:, K_Q, hslice(head, t128)]
        k_ap = qkvs[bs][:, K_K, hslice(head, t128)]
        v_ap = qkvs[bs][:, K_V, hslice(head, t128)]
        bqk = bv["qkvs"][bs]

        bo = 384 * hb

        def st0():
            c.mm_group(bv["r_kt"][hb], [(lambda e: e.transpose(
                ps_bf[:, bo + 0:bo + 128], k_ap, ids[:]), [bqk, bv["ids"]])])
            c.op(c.act, lambda e: e.activation(out=kts[hb][:], in_=ps_bf[:, bo + 0:bo + 128], func=AF.Identity),
                 reads=[bv["r_kt"][hb]], writes=[bv["kts"][hb]], selfw=False)
            c.mm_group(bv["r_vt"][hb], [(lambda e: e.transpose(
                ps_bf[:, bo + 128:bo + 256], v_ap, ids[:]), [bqk, bv["ids"]])])
            c.op(c.act, lambda e: e.activation(out=vts[hb][:], in_=ps_bf[:, bo + 128:bo + 256], func=AF.Identity),
                 reads=[bv["r_vt"][hb]], writes=[bv["vts"][hb]], selfw=False)
            if not first:
                c.op(c.act, lambda e: e.activation(
                    out=s1sb[hb][:], in_=ps_states[:, s1col], func=AF.Identity),
                    reads=[bv["ps_states"]], writes=[bv["s1sb"][hb]], selfw=False)
                c.op(c.act, lambda e: e.activation(
                    out=s2sb[hb][:], in_=ps_states[0:64, s2col], func=AF.Identity),
                    reads=[bv["ps_states"]], writes=[bv["s2sb"][hb]], selfw=False)

        def st1():
            c.mm_group(bv["r_at"][hb], [(lambda e: e.matmul(
                ps_f1[hb][:, 0:128], k_ap, q_ap, start=True, stop=True,
                skip_group_check=True), [bqk])])
            c.op(c.dve, lambda e: e.tensor_tensor(
                out=ats[hb][:], in0=trs[:], in1=ps_f1[hb][:, 0:128], op=OP.mult),
                reads=[bv["trs"], bv["r_at"][hb]], writes=[bv["ats"][hb]], selfw=False)
            if hb == 0:
                c.mm_group(bv["r_cum"], [
                    (lambda e: e.matmul(ps_f2[0][:, 0:128], trs[:],
                                        estm[bs][:, tc, :], start=True,
                                        stop=False, skip_group_check=True),
                     [bv["trs"], bv["estm"][bs]]),
                    (lambda e: e.matmul(ps_f2[0][:, 0:128], onesb_row[:], c1b[:],
                                        start=False, stop=True,
                                        skip_group_check=True),
                     [bv["ones"], bv["c1b"]]),
                ])
                c.op(c.dve, lambda e: e.reciprocal(out=enzc[:], in_=ps_f2[0][:, 0:128]),
                     reads=[bv["r_cum"]], writes=[bv["enzc"]], selfw=False)
                c.mm_group(bv["r_col"], [(lambda e: e.matmul(
                    ps_f2[0][0:1, 128:256], onesb_col[:], estm[bs][:, tc, :],
                    start=True, stop=True, skip_group_check=True),
                    [bv["ones"], bv["estm"][bs]])])
                c.op(c.dve, lambda e: e.tensor_tensor(
                    out=c1f[:], in0=c1f[:], in1=ps_f2[0][0:1, 128:256],
                    op=OP.add),
                    reads=[bv["r_col"], bv["c1f"]], writes=[bv["c1f"]], selfw=False)
                c.op(c.act, lambda e: e.activation(out=c1b[:], in_=c1f[:],
                     func=AF.Identity),
                     reads=[bv["c1f"]], writes=[bv["c1b"]], selfw=False)

        ok_reg = (ps_f2[0], slice(256, 320)) if hb == 0 else (ps_f2[1], slice(0, 64))

        def st2():
            okp, oks = ok_reg
            parts = []
            if not first:
                parts.append((lambda e: e.matmul(
                    okp[:, oks], q_ap, s1sb[hb][:], start=True, stop=False,
                    skip_group_check=True), [bqk, bv["s1sb"][hb]]))
            parts.append((lambda e, st=first: e.matmul(
                okp[:, oks], ats[hb][:], estm[bs][:, tc, es_m],
                start=st, stop=True, skip_group_check=True),
                [bv["ats"][hb], bv["estm"][bs]]))
            c.mm_group(bv["r_ok"][hb], parts)
            c.op(c.dve, lambda e: e.tensor_tensor(
                out=okf[hb][:], in0=enzc[:, es_m], in1=okp[:, oks], op=OP.mult),
                reads=[bv["enzc"], bv["r_ok"][hb]], writes=[bv["okf"][hb]], selfw=False)
            c.op(c.dve, lambda e: e.reduce_max(
                out=mxs[hb][:, 0:1], in_=okf[hb][:], axis=AX, negate=True),
                reads=[bv["okf"][hb]], writes=[bv["mxs"][hb]], selfw=False)
            c.op(c.act, lambda e: e.activation(
                out=exf[hb][:], in_=okf[hb][:], func=AF.Exp, bias=mxs[hb][:, 0:1]),
                reads=[bv["okf"][hb], bv["mxs"][hb]], writes=[bv["exf"][hb]])
            c.op(c.dve, lambda e: e.reduce_sum(
                out=mxs[hb][:, 1:2], in_=exf[hb][:], axis=AX),
                reads=[bv["exf"][hb]], writes=[bv["mxs"][hb]], selfw=False)
            c.op(c.dve, lambda e: e.reciprocal(out=mxs[hb][:, 0:1], in_=mxs[hb][:, 1:2]),
                 reads=[bv["mxs"][hb]], writes=[bv["mxs"][hb]])
            c.op(c.dve, lambda e: e.tensor_scalar(
                out=exf[hb][:], in0=exf[hb][:], scalar1=mxs[hb][:, 0:1],
                scalar2=None, op0=OP.mult),
                reads=[bv["mxs"][hb], bv["exf"][hb]], writes=[bv["exf"][hb]])
            c.op(c.dve, lambda e: e.tensor_tensor(
                out=qeps[hb][:, 0:64], in0=exf[hb][:], in1=enzc[:, es_m],
                op=OP.mult),
                reads=[bv["exf"][hb], bv["enzc"]], writes=[bv["qeps"][hb]], selfw=False)

        def st3():
            c.mm_group(bv["r_qet"][hb], [(lambda e: e.transpose(
                ps_bf[:, bo + 256:bo + 384], qeps[hb][:], ids[:]),
                [bv["qeps"][hb], bv["ids"]])])
            c.op(c.act, lambda e: e.activation(
                out=qets[hb][:], in_=ps_bf[0:64, bo + 256:bo + 384], func=AF.Identity),
                reads=[bv["r_qet"][hb]], writes=[bv["qets"][hb]], selfw=False)
            c.mm_group(bv["r_w2"][hb], [(lambda e: e.matmul(
                ps_f1[hb][:, 128:256], esmm[bs][head][:, t128], qets[hb][:],
                start=True, stop=True, skip_group_check=True),
                [bv["esmm"][bs][head], bv["qets"][hb]])])
            c.op(c.dve, lambda e: e.tensor_tensor(
                out=w2s[hb][:], in0=trs[:], in1=ps_f1[hb][:, 128:256], op=OP.mult),
                reads=[bv["trs"], bv["r_w2"][hb]], writes=[bv["w2s"][hb]], selfw=False)

        def st4():
            parts = []
            if not first:
                parts.append((lambda e: e.matmul(
                    ps_f1[hb][:, 256:384], s2sb[hb][:], qets[hb][:],
                    start=True, stop=False, skip_group_check=True),
                    [bv["s2sb"][hb], bv["qets"][hb]]))
            parts.append((lambda e, st=first: e.matmul(
                ps_f1[hb][:, 256:384], vts[hb][:], w2s[hb][:], start=st, stop=True,
                skip_group_check=True), [bv["vts"][hb], bv["w2s"][hb]]))
            c.mm_group(bv["r_o"][hb], parts)
            c.mm_group(bv["ps_states"], [(lambda e: e.matmul(
                ps_states[:, s1col], kts[hb][:], estm[bs][:, tc, es_m],
                start=False, stop=last, skip_group_check=True),
                [bv["kts"][hb], bv["estm"][bs]])])
            c.mm_group(bv["ps_states"], [(lambda e: e.matmul(
                ps_states[0:64, s2col], estm[bs][:, tc, es_m], vts[hb][:],
                start=False, stop=last, skip_group_check=True),
                [bv["estm"][bs], bv["vts"][hb]])])

        def st5():
            c.op(c.act, lambda e: e.activation(
                out=oraw[hb][:, t128], in_=ps_f1[hb][:, 256:384], func=AF.Identity),
                reads=[bv["r_o"][hb]], writes=[bv["oraw"][hb]], selfw=False)

        return [st0, st1, st2, st3, st4, st5]

    units = [(b, h2) for b in range(B) for h2 in range(2)][:nsu]
    pending = unit_items(*units[0], 0)
    for it in pending:
        it()
    pending = []
    for ui, (b, h2) in enumerate(units):
        bs = ui % 2
        tbase = b * T + h2 * 2048
        if h2 == 0:
            c.op(c.dve, lambda e: e.memset(c1f[:], 0.0), writes=[bv["c1f"]])
            c.op(c.dve, lambda e: e.memset(c1b[:], 0.0), writes=[bv["c1b"]])
            c.mm_group(bv["ps_states"], [(lambda e: e.matmul(
                ps_states[:, :], zcol[:], zrow[:], start=True, stop=False,
                skip_group_check=True), [bv["ones"]])])
        if ui + 1 < len(units):
            pending = unit_items(*units[ui + 1], (ui + 1) % 2)
        else:
            pending = []
        if not ilv:
            while pending:
                pending.pop(0)()
        else:
            tail = pending[ilv_n:]
            pending = pending[:ilv_n]
            for it in tail:
                pass
            self_tail = tail
        if phases == "scan_only_check":
            pending = []
        for tc in range(16):
            first = (h2 == 0 and tc == 0)
            last = (h2 == 1 and tc == 15)
            stA = chunk_stages(0, tc, first, last, bs)
            stB = chunk_stages(1, tc, first, last, bs)
            for s in range(6):
                if str(s) not in stmask:
                    continue
                stA[s]()
                stB[s]()
            for _ in range(5):
                if pending:
                    pending.pop(0)()
        while pending:
            pending.pop(0)()
        if ilv and 'self_tail' in dir():
            pass
        if ilv:
            for it in self_tail:
                it()
            self_tail = []
        for head in range(2):
            hb = head
            for w4 in range(4):
                w512 = slice(512 * w4, 512 * (w4 + 1))
                c.op(c.act, lambda e, w512=w512, hb=hb: e.activation(
                    out=onw[:], in_=oraw[hb][:, w512], func=AF.Square),
                    reads=[bv["oraw"][hb]], writes=[bv["onw"]])
                c.mm_group([bv["ps_in"][hb]], [(lambda e, hb=hb: e.matmul(
                    ps_in[hb][0:1, 0:512], ones_col[:], onw[:],
                    start=True, stop=True, skip_group_check=True),
                    [bv["ones"], bv["onw"]])])
                c.op(c.act, lambda e, w4=w4, hb=hb: e.activation(
                    out=rbw[0:1, 0:512],
                    in_=ps_in[hb][0:1, 0:512], func=AF.Ln,
                    scale=1.0 / DV, bias=eps_c),
                    reads=[bv["ps_in"][hb], c.const_buf], writes=[bv["rbw"]])
                c.op(c.act, lambda e, w4=w4, hb=hb: e.activation(
                    out=rnb[hb][0:1, 512 * w4:512 * (w4 + 1)],
                    in_=rbw[0:1, 0:512], func=AF.Exp, scale=-0.5),
                    reads=[bv["rbw"]], writes=[bv["rnb"][hb]])
            for w4 in range(4):
                w512 = slice(512 * w4, 512 * (w4 + 1))
                c.mm_group([bv["ps_in"][hb]], [(lambda e, w512=w512, hb=hb: e.matmul(
                    ps_in[hb][:, 0:512], gnwrb[:], rnb[hb][0:1, w512],
                    start=True, stop=True, skip_group_check=True),
                    [bv["gnws"], bv["rnb"][hb]])])
                c.op(c.dve, lambda e, hb=hb: e.tensor_copy(
                    out=rbw[:], in_=ps_in[hb][:, 0:512]),
                    reads=[bv["ps_in"][hb]], writes=[bv["rbw"]])
                c.op(c.dve, lambda e, w512=w512, hb=hb: e.tensor_tensor(
                    out=onw[:], in0=rbw[:], in1=oraw[hb][:, w512], op=OP.mult),
                    reads=[bv["rbw"], bv["oraw"][hb]], writes=[bv["onw"]])
                c.op(c.dve, lambda e, w512=w512, hb=hb, head=head, bs=bs:
                     e.tensor_tensor(
                        out=ogsb[:, head, w512], in0=onw[:],
                        in1=qkvs[bs][:, K_S, hslice(head, w512)], op=OP.mult),
                     reads=[bv["onw"], bv["qkvs"][bs]], writes=[bv["ogsb"]])
        for head in range(2):
            c.op(c.pool, lambda e, head=head, tbase=tbase: e.dma_start(
                out=ogo[128 * head:128 * (head + 1), tbase:tbase + 2048],
                in_=ogsb[:, head, :]), reads=[bv["ogsb"]], inc=16)

    c.finalize_queues()
    return c.build()



def _packed_T(w_packed):
    b = np.ascontiguousarray(w_packed).view(np.uint8).reshape(
        w_packed.shape[0], w_packed.shape[1], 4)[:, :, 0]
    return np.ascontiguousarray(b.T)


def _f32T(w):
    return np.ascontiguousarray(np.asarray(w, np.float32).T)


def _rope_tables():
    pos = np.arange(T, dtype=np.float32)
    inv = (10000.0 ** (-(np.arange(0, DK, 2, dtype=np.float32)) / DK))
    ang = inv[:, None] * pos[None, :]
    cos = np.cos(ang).astype(np.float32)
    sin = np.sin(ang).astype(np.float32)
    cosd = np.concatenate([cos, cos], axis=0)
    sins = np.concatenate([sin, -sin], axis=0)
    return cosd, sins


def _l2_const_inputs():
    cosd, sins = _rope_tables()
    triu = np.triu(np.ones((128, 128), np.float32)).astype(BF)
    idn = np.eye(128, dtype=np.float32).astype(BF)
    return cosd.astype(BF), sins.astype(BF), triu, idn


def _l2_maps(xln2_full, wq_b, wk_b, wv_b, wga_b, ws1_b, ws2_b, bs2, gnwv):
    cd, sd, triu, idn = _l2_const_inputs()
    bs2 = np.asarray(bs2, np.float32)
    maps = []
    for cid in range(N_CORES):
        hc = slice(256 * cid, 256 * (cid + 1))
        mc = slice(128 * cid, 128 * (cid + 1))
        maps.append({
            "xall": xln2_full,
            "wq": np.ascontiguousarray(wq_b[:, hc]),
            "wk": np.ascontiguousarray(wk_b[:, hc]),
            "wv": np.ascontiguousarray(wv_b[:, hc]),
            "wga": np.ascontiguousarray(wga_b[:, hc]),
            "ws1": ws1_b,
            "ws2": np.ascontiguousarray(ws2_b[:, mc]),
            "bs2A": np.ascontiguousarray(bs2[128 * cid:128 * cid + 64][:, None]),
            "bs2B": np.ascontiguousarray(bs2[128 * cid + 64:128 * cid + 128][:, None]),
            "bs2r": np.ascontiguousarray(bs2[mc][None, :]),
            "gnwr": np.ascontiguousarray(np.asarray(gnwv, np.float32)[None, :]),
            "cosd": cd, "sind": sd,
            "triu": triu, "idn": idn,
        })
    return maps


def kernel(
    L0_attn_hidden_states, L0_attn_Wq, L0_attn_Wk, L0_attn_Wv, L0_attn_Wg,
    L0_attn_Wo, L0_attn_Ws1, L0_attn_Ws2, L0_attn_bs2, L0_attn_g_norm_weight,
    L0_ffn_gate_W_packed, L0_ffn_gate_scales, L0_ffn_up_W_packed,
    L0_ffn_up_scales, L0_ffn_down_W_packed, L0_ffn_down_scales,
):
    pk_g = _packed_T(L0_ffn_gate_W_packed)
    pk_u = _packed_T(L0_ffn_up_W_packed)
    pk_d = _packed_T(L0_ffn_down_W_packed)
    sc_g = _f32T(L0_ffn_gate_scales)
    sc_u = _f32T(L0_ffn_up_scales)
    sc_d = _f32T(L0_ffn_down_scales)
    wqT = _f32T(L0_attn_Wq)
    wkT = _f32T(L0_attn_Wk)
    wvT = _f32T(L0_attn_Wv)
    wgaT = _f32T(L0_attn_Wg)
    woT = _f32T(L0_attn_Wo)
    ws1T = _f32T(L0_attn_Ws1)
    ws2T = _f32T(L0_attn_Ws2)
    p8 = np.zeros((8, 128), np.float32)
    for r in range(8):
        p8[r, 16 * r:16 * (r + 1)] = 1.0

    nc0 = build_l0()
    maps0 = []
    for cid in range(N_CORES):
        maps0.append({
            "pk_g": pk_g[128 * cid:128 * (cid + 1)],
            "pk_u": pk_u[128 * cid:128 * (cid + 1)],
            "pk_d": pk_d[1024 * cid:1024 * (cid + 1)],
            "sc_g": sc_g[8 * cid:8 * (cid + 1)],
            "sc_u": sc_u[8 * cid:8 * (cid + 1)],
            "sc_d": sc_d[64 * cid:64 * (cid + 1)],
            "p8": p8,
            "wq": wqT[256 * cid:256 * (cid + 1)],
            "wk": wkT[256 * cid:256 * (cid + 1)],
            "wv": wvT[256 * cid:256 * (cid + 1)],
            "wga": wgaT[256 * cid:256 * (cid + 1)],
            "wo": woT[256 * cid:256 * (cid + 1)],
            "ws1": ws1T,
            "ws2": ws2T,
        })
    res0 = run_bass_kernel_spmd(nc0, maps0, core_ids=list(range(N_CORES)))
    r0 = res0.results
    WgT = np.concatenate([r0[c]["o_g"] for c in range(N_CORES)], axis=0)
    WuT = np.concatenate([r0[c]["o_u"] for c in range(N_CORES)], axis=0)
    WdT = np.concatenate([r0[c]["o_d"] for c in range(N_CORES)], axis=0)
    wq_b = np.concatenate([r0[c]["owq"] for c in range(N_CORES)], axis=0)
    wk_b = np.concatenate([r0[c]["owk"] for c in range(N_CORES)], axis=0)
    wv_b = np.concatenate([r0[c]["owv"] for c in range(N_CORES)], axis=0)
    wga_b = np.concatenate([r0[c]["owga"] for c in range(N_CORES)], axis=0)
    wo_b = np.concatenate([r0[c]["owo"] for c in range(N_CORES)], axis=0)
    ws1_b = r0[0]["o_ws1"]
    ws2_b = r0[0]["o_ws2"]

    x0T = np.ascontiguousarray(
        np.asarray(L0_attn_hidden_states, np.float32).reshape(B * T, D).T)
    nc1 = build_ffn_launch(is_l3=False)
    maps1 = [{
        "xin": np.ascontiguousarray(x0T[:, TC * cid:TC * (cid + 1)]),
        "wgt": WgT, "wut": WuT, "wdt": WdT,
    } for cid in range(N_CORES)]
    res1 = run_bass_kernel_spmd(nc1, maps1, core_ids=list(range(N_CORES)))
    x1s = [res1.results[cid]["xout"] for cid in range(N_CORES)]
    xln2_full = np.concatenate(
        [res1.results[cid]["xlnout"] for cid in range(N_CORES)], axis=1)

    nc2 = build_l2()
    maps2 = _l2_maps(xln2_full, wq_b, wk_b, wv_b, wga_b, ws1_b, ws2_b,
                     L0_attn_bs2, L0_attn_g_norm_weight)
    res2 = run_bass_kernel_spmd(nc2, maps2, core_ids=list(range(N_CORES)))
    og_full = np.concatenate(
        [res2.results[cid]["ogo"] for cid in range(N_CORES)], axis=0)

    nc3 = build_ffn_launch(is_l3=True)
    maps3 = [{
        "xin": x1s[cid],
        "og": np.ascontiguousarray(og_full[:, TC * cid:TC * (cid + 1)]),
        "wot": wo_b,
        "wgt": WgT, "wut": WuT, "wdt": WdT,
    } for cid in range(N_CORES)]
    res3 = run_bass_kernel_spmd(nc3, maps3, core_ids=list(range(N_CORES)))
    x3 = np.concatenate(
        [res3.results[cid]["xout"] for cid in range(N_CORES)], axis=1)
    return np.ascontiguousarray(x3.T).reshape(B, T, D)


# version 07 (final)
# speedup vs baseline: 1.0256x; 1.0256x over previous
import contextlib
import math

import numpy as np
import ml_dtypes

import concourse.bass as bass
import concourse.mybir as mybir
from concourse.bass_utils import run_bass_kernel_spmd
from concourse.alu_op_type import AluOpType as OP

F32 = mybir.dt.float32
BF16 = mybir.dt.bfloat16
I32 = mybir.dt.int32
U8 = mybir.dt.uint8
AF = mybir.ActivationFunctionType
AX = mybir.AxisListType.X
BF = ml_dtypes.bfloat16

N_CORES = 8
D = 2048
MLP = 16384
B = 2
T = 4096
TC = (B * T) // N_CORES
H = 16
DK = 128
DV = 128
M = 64
HPC = H // N_CORES * B
EPS = 1e-5
CLAMP = 32.0
SCALE = float(DK) ** -0.5
GELU_B = 0.7978845608028654
GELU_C = 0.044715



class Sem:
    def __init__(self, name):
        self.name = name
        self.h = None
        self.n = 0


class Buf:
    __slots__ = ("writer", "readers", "name", "bank")

    def __init__(self, name="", bank=None):
        self.writer = None
        self.readers = []
        self.name = name
        self.bank = bank


class Bank:
    __slots__ = ("pe_tag", "readers")

    def __init__(self):
        self.pe_tag = None
        self.readers = []


class Eng:
    def __init__(self, ctx, name, sem, is_dma_queue=False, sem_pool=None):
        self.ctx = ctx
        self.name = name
        self.sem = sem
        self.sem_pool = sem_pool
        self.rr = 0
        self.is_dma_queue = is_dma_queue
        self.items = []
        self.satisfied = {}

    def wait(self, sem, count, allow_self=True):
        if count <= 0:
            return
        if sem is self.sem and self.name == "pe":
            return
        if sem is self.sem and not allow_self:
            return
        if self.satisfied.get(sem, 0) >= count:
            return
        self.items.append(("wait", (sem, count)))
        self.satisfied[sem] = count


class Ctx:

    def __init__(self, name):
        self.nc = bass.Bass("TRN2", target_bir_lowering=False)
        self.name = name
        self.stack = contextlib.ExitStack()
        self.sems = []
        self.pe = self._eng("pe")
        self.dve = self._eng("dve")
        self.act = self._eng("act")
        self.pool = self._eng("pool", dma=True)
        self.qin = self._eng("qin", dma=True)
        self._const_vals = {}
        self._const_tile = None
        self._n_const = 0
        self._n_psum = 0
        self.selfw_default = True

    def _eng(self, name, dma=False):
        s = Sem(name + "_sem")
        self.sems.append(s)
        pool = None
        if dma:
            pool = [Sem(f"{name}_s{i}") for i in range(8)]
            self.sems.extend(pool)
        return Eng(self, name, s, is_dma_queue=dma, sem_pool=pool)

    def sbuf(self, name, shape, dtype):
        t = self.stack.enter_context(self.nc.sbuf_tensor(name, list(shape), dtype))
        return t

    def psum(self, shape, dtype=F32, name=None):
        if name is None:
            name = f"ps{self._n_psum}"
            self._n_psum += 1
        return self.stack.enter_context(
            self.nc.psum_tensor(name, list(shape), dtype))

    def dram_in(self, name, shape, dtype):
        return self.nc.dram_tensor(name, list(shape), dtype, kind="ExternalInput").ap()

    def dram_out(self, name, shape, dtype):
        return self.nc.dram_tensor(name, list(shape), dtype, kind="ExternalOutput").ap()

    def const_cell(self, val, p=128):
        if self._const_tile is None:
            self._const_tile = self.sbuf("consts", [128, 32], F32)
            self.const_buf = Buf("consts")
        key = float(val)
        if key not in self._const_vals:
            i = self._n_const
            assert i < 32
            self._n_const += 1
            col = self._const_tile[:, i:i + 1]
            self.op(self.dve, lambda e, col=col, v=key: e.memset(col, v),
                    writes=[self.const_buf])
            self._const_vals[key] = i
        i = self._const_vals[key]
        return self._const_tile[0:p, i:i + 1]

    def op(self, eng, fn, reads=(), writes=(), inc=1, tag_reads=True, selfw=None):
        if selfw is None:
            selfw = self.selfw_default
        for b in reads:
            if b.writer:
                eng.wait(*b.writer, allow_self=selfw)
        for b in writes:
            if b.writer:
                eng.wait(*b.writer)
            for r in b.readers:
                eng.wait(*r)
        banks_read = []
        if eng is not self.pe:
            for b in reads:
                if getattr(b, "bank", None) is not None:
                    if b.bank.pe_tag:
                        eng.wait(*b.bank.pe_tag)
                    banks_read.append(b.bank)
        if inc:
            if eng.sem_pool is not None and inc == 16:
                sem = eng.sem_pool[eng.rr % len(eng.sem_pool)]
                eng.rr += 1
                if sem.n:
                    eng.wait(sem, sem.n)
                sem.n += 16
                tag = (sem, sem.n)
                eng.items.append(("op", (fn, sem, 16)))
            else:
                eng.sem.n += inc
                tag = (eng.sem, eng.sem.n)
                eng.items.append(("op", (fn, eng.sem, inc)))
        else:
            tag = None
            eng.items.append(("op", (fn, None, 0)))
        if tag is not None:
            for b in writes:
                b.writer = tag
                b.readers = []
            if tag_reads:
                for b in reads:
                    b.readers.append(tag)
            for bk in banks_read:
                bk.readers.append(tag)
        return tag

    def finalize_queues(self):
        for eng in (self.pool, self.qin):
            for s in (eng.sem_pool or []):
                if s.n:
                    eng.items.append(("wait", (s, s.n)))
            if eng.sem.n:
                eng.items.append(("wait", (eng.sem, eng.sem.n)))

    def mm_group(self, psum_buf, parts, inc=1):
        eng = self.pe
        pbufs = psum_buf if isinstance(psum_buf, (list, tuple)) else [psum_buf]
        allreads = []
        for fn, reads in parts:
            for b in reads:
                if b.writer:
                    eng.wait(*b.writer)
                allreads.append(b)
        banks = []
        for pb in pbufs:
            if pb.writer:
                eng.wait(*pb.writer)
            for r in pb.readers:
                eng.wait(*r)
            bk = getattr(pb, "bank", None)
            if bk is not None and bk not in banks:
                banks.append(bk)
        for bk in banks:
            for r in bk.readers:
                eng.wait(*r)
        for i, (fn, reads) in enumerate(parts):
            last = i == len(parts) - 1
            if last:
                eng.sem.n += inc
                eng.items.append(("op", (fn, eng.sem, inc)))
            else:
                eng.items.append(("op", (fn, None, 0)))
        tag = (eng.sem, eng.sem.n)
        for pb in pbufs:
            pb.writer = tag
            pb.readers = []
        for bk in banks:
            bk.pe_tag = tag
            bk.readers = []
        for b in allreads:
            b.readers.append(tag)
        return tag

    def build(self):
        nc = self.nc
        for s in self.sems:
            s.h = self.stack.enter_context(nc.semaphore(s.name))
        block = self.stack.enter_context(nc.Block())

        def replay(items):
            def run(raw):
                for kind, payload in items:
                    if kind == "wait":
                        sem, count = payload
                        raw.wait_ge(sem.h, count)
                    else:
                        fn, sem, inc = payload
                        instr = fn(raw)
                        if sem is not None:
                            instr.then_inc(sem.h, inc)
            return run

        block.tensor(replay(self.pe.items))
        block.vector(replay(self.dve.items))
        block.scalar(replay(self.act.items))
        block.gpsimd(replay(self.pool.items))
        block.sync(replay(self.qin.items))
        self.stack.close()
        return nc



def build_l0():
    c = Ctx("l0")
    c.selfw_default = False
    nc = c.nc

    pk_g = c.dram_in("pk_g", [128, MLP], U8)
    pk_u = c.dram_in("pk_u", [128, MLP], U8)
    pk_d = c.dram_in("pk_d", [1024, D], U8)
    sc_g = c.dram_in("sc_g", [8, MLP], F32)
    sc_u = c.dram_in("sc_u", [8, MLP], F32)
    sc_d = c.dram_in("sc_d", [64, D], F32)
    p8 = c.dram_in("p8", [8, 128], F32)
    w_in = {}
    for nm in ("wq", "wk", "wv", "wga", "wo"):
        w_in[nm] = c.dram_in(nm, [256, D], F32)
    ws1 = c.dram_in("ws1", [D, 16], F32)
    ws2 = c.dram_in("ws2", [16, 1024], F32)

    o_g = c.dram_out("o_g", [256, MLP], BF16)
    o_u = c.dram_out("o_u", [256, MLP], BF16)
    o_d = c.dram_out("o_d", [D, D], BF16)
    w_out = {}
    for nm in ("wq", "wk", "wv", "wga", "wo"):
        w_out[nm] = c.dram_out("o" + nm, [256, D], BF16)
    o_ws1 = c.dram_out("o_ws1", [D, 16], BF16)
    o_ws2 = c.dram_out("o_ws2", [16, 1024], BF16)

    W = 1024

    def pair(nm, shape, dt):
        return [c.sbuf(nm + "0", shape, dt), c.sbuf(nm + "1", shape, dt)]

    vi = pair("vi", [128, W], I32)
    l3 = pair("l3", [128, W], I32)
    h3 = pair("h3", [128, W], I32)
    bit = pair("bit", [128, W], I32)
    zl = pair("zl", [128, W], F32)
    zh = pair("zh", [128, W], F32)
    xf = pair("xf", [128, W], F32)
    r1 = pair("r1", [128, W], F32)
    r2 = pair("r2", [128, W], F32)
    mag = pair("mag", [128, W], F32)
    val = pair("val", [128, W], F32)
    wout = [c.sbuf("wout0", [128, 2, W], BF16), c.sbuf("wout1", [128, 2, W], BF16)]
    scx = pair("scx", [128, W], F32)
    p8t = c.sbuf("p8t", [8, 128], F32)
    scb = c.sbuf("scb", [8, MLP], F32)
    ps_sc = [c.psum([128, 512], F32, name="ps_sc0"),
             c.psum([128, 512], F32, name="ps_sc1")]

    bufs = {}
    for nm in ("vi", "l3", "h3", "bit", "zl", "zh", "xf", "r1", "r2", "mag",
               "val", "wout", "scx", "ps_sc"):
        bufs[nm] = [Buf(nm + "0"), Buf(nm + "1")]
    for nm in ("p8t", "scb"):
        bufs[nm] = Buf(nm)
    bv = bufs
    bank_sc = [Bank(), Bank()]
    bv["ps_sc"][0].bank = bank_sc[0]
    bv["ps_sc"][1].bank = bank_sc[1]

    c.op(c.qin, lambda e: e.dma_start(out=p8t[:], in_=p8[:]), writes=[bv["p8t"]], inc=16)
    m4 = c.const_cell(-4.0)
    m6 = c.const_cell(-6.0)

    def tile_items(s, pk_ap, out_ap, sc_off, neg_half):
        width = W
        items = []
        zsc = -1.0 if neg_half else -2.0
        zof = 0.5 if neg_half else 1.0
        it = items.append
        it(lambda: c.op(c.pool, lambda e: e.dma_start(out=vi[s][:], in_=pk_ap),
                        writes=[bv["vi"][s]], inc=16))
        it(lambda: c.op(c.dve, lambda e: e.tensor_scalar(out=l3[s][:], in0=vi[s][:],
           scalar1=7, scalar2=None, op0=OP.bitwise_and),
           reads=[bv["vi"][s]], writes=[bv["l3"][s]]))
        it(lambda: c.op(c.dve, lambda e: e.tensor_scalar(out=h3[s][:], in0=vi[s][:],
           scalar1=4, scalar2=7, op0=OP.logical_shift_right, op1=OP.bitwise_and),
           reads=[bv["vi"][s]], writes=[bv["h3"][s]]))
        it(lambda: c.op(c.dve, lambda e: e.tensor_scalar(out=bit[s][:], in0=vi[s][:],
           scalar1=3, scalar2=1, op0=OP.logical_shift_right, op1=OP.bitwise_and),
           reads=[bv["vi"][s]], writes=[bv["bit"][s]]))
        it(lambda: c.op(c.dve, lambda e: e.tensor_scalar(
           out=zl[s][:], in0=bit[s][:],
           scalar1=zsc, scalar2=zof, op0=OP.mult, op1=OP.add),
           reads=[bv["bit"][s]], writes=[bv["zl"][s]]))
        it(lambda: c.op(c.dve, lambda e: e.tensor_scalar(out=bit[s][:], in0=vi[s][:],
           scalar1=7, scalar2=1, op0=OP.logical_shift_right, op1=OP.bitwise_and),
           reads=[bv["vi"][s]], writes=[bv["bit"][s]]))
        it(lambda: c.op(c.dve, lambda e: e.tensor_scalar(
           out=zh[s][:], in0=bit[s][:],
           scalar1=zsc, scalar2=zof, op0=OP.mult, op1=OP.add),
           reads=[bv["bit"][s]], writes=[bv["zh"][s]]))
        for s0 in (0, 512):
            it(lambda s0=s0: (c.mm_group(bv["ps_sc"][s], [
                (lambda e: e.matmul(ps_sc[s][:, :],
                    p8t[:], scb[:, sc_off + s0:sc_off + s0 + 512],
                    start=True, stop=True, skip_group_check=True),
                 [bv["p8t"], bv["scb"]])]),
                c.op(c.dve, lambda e: e.tensor_copy(
                    out=scx[s][:, s0:s0 + 512], in_=ps_sc[s][:, :]),
                    reads=[bv["ps_sc"][s]], writes=[bv["scx"][s]]))[-1])
        for parity in range(2):
            code = l3[s] if parity == 0 else h3[s]
            cbuf = bv["l3" if parity == 0 else "h3"][s]
            zb = zl[s] if parity == 0 else zh[s]
            zbuf = bv["zl" if parity == 0 else "zh"][s]
            it(lambda code=code, cbuf=cbuf: c.op(c.dve, lambda e: e.tensor_copy(
               out=xf[s][:], in_=code[:]), reads=[cbuf], writes=[bv["xf"][s]]))
            it(lambda: c.op(c.act, lambda e: e.activation(out=r1[s][:], in_=xf[s][:],
               func=AF.Relu, scale=1.0, bias=m4),
               reads=[bv["xf"][s], c.const_buf], writes=[bv["r1"][s]]))
            it(lambda: c.op(c.act, lambda e: e.activation(out=r2[s][:], in_=xf[s][:],
               func=AF.Relu, scale=1.0, bias=m6),
               reads=[bv["xf"][s], c.const_buf], writes=[bv["r2"][s]]))
            it(lambda: c.op(c.dve, lambda e: e.scalar_tensor_tensor(out=mag[s][:],
               in0=r1[s][:], scalar=0.5, in1=r2[s][:],
               op0=OP.mult, op1=OP.add),
               reads=[bv["r1"][s], bv["r2"][s]], writes=[bv["mag"][s]]))
            it(lambda: c.op(c.dve, lambda e: e.scalar_tensor_tensor(out=mag[s][:],
               in0=xf[s][:], scalar=0.5, in1=mag[s][:],
               op0=OP.mult, op1=OP.add),
               reads=[bv["xf"][s], bv["mag"][s]], writes=[bv["mag"][s]]))
            it(lambda zb=zb, zbuf=zbuf: c.op(c.pool, lambda e: e.tensor_tensor(
               out=val[s][:], in0=zb[:], in1=mag[s][:], op=OP.mult),
               reads=[zbuf, bv["mag"][s]], writes=[bv["val"][s]], inc=1))
            it(lambda parity=parity: c.op(c.dve, lambda e: e.tensor_tensor(
               out=wout[s][:, parity, :], in0=val[s][:],
               in1=scx[s][:], op=OP.mult),
               reads=[bv["val"][s], bv["scx"][s]], writes=[bv["wout"][s]]))
        it(lambda: c.op(c.pool, lambda e: e.dma_start(out=out_ap, in_=wout[s][:, :, :]),
                        reads=[bv["wout"][s]], inc=16))
        return items

    tiles = []
    ov_g = o_g.rearrange("(kb two) n -> kb two n", two=2)
    ov_u = o_u.rearrange("(kb two) n -> kb two n", two=2)
    for mat, (pk, scin, ov) in enumerate(((pk_g, sc_g, ov_g), (pk_u, sc_u, ov_u))):
        for n0 in range(0, MLP, W):
            tiles.append(("gu", mat, pk[:, n0:n0 + W], ov[:, :, n0:n0 + W], n0))
    od = o_d.rearrange("(t kb two) n -> t kb two n", t=8, two=2)
    for t in range(8):
        for w0 in range(0, D, W):
            tiles.append(("d", 2, pk_d[128 * t:128 * (t + 1), w0:w0 + W],
                          od[t][:, :, w0:w0 + W], t * D + w0))

    cur_mat = [-1]

    def load_scales(mat):
        if mat == cur_mat[0]:
            return
        cur_mat[0] = mat
        if mat == 0:
            c.op(c.qin, lambda e: e.dma_start(out=scb[:], in_=sc_g[:]),
                 writes=[bv["scb"]], inc=16)
        elif mat == 1:
            c.op(c.qin, lambda e: e.dma_start(out=scb[:], in_=sc_u[:]),
                 writes=[bv["scb"]], inc=16)
        else:
            c.op(c.qin, lambda e: e.dma_start(
                out=scb[:, :8 * D].rearrange("p (t n) -> p t n", n=D),
                in_=sc_d.rearrange("(t p) n -> p t n", p=8)),
                writes=[bv["scb"]], inc=16)

    carry = []
    for i, (kind, mat, pk_ap, out_ap, sc_off) in enumerate(tiles):
        load_scales(mat)
        cur = tile_items(i % 2, pk_ap, out_ap, sc_off, neg_half=(kind == "d"))
        half = len(cur) // 2
        head, tail = cur[:half], cur[half:]
        n = max(len(carry), len(head))
        for j in range(n):
            if j < len(carry):
                carry[j]()
            if j < len(head):
                head[j]()
        carry = tail
    for f in carry:
        f()

    wtile = val[0]
    wbf = wout[0]
    for nm in ("wq", "wk", "wv", "wga", "wo"):
        for r in range(2):
            for hw in range(2):
                csl = slice(1024 * hw, 1024 * (hw + 1))
                c.op(c.qin, lambda e, nm=nm, r=r, csl=csl: e.dma_start(
                    out=wtile[:], in_=w_in[nm][128 * r:128 * (r + 1), csl]),
                    writes=[bv["val"][0]], inc=16)
                c.op(c.dve, lambda e: e.tensor_copy(
                    out=wbf[:, 0, :], in_=wtile[:]),
                    reads=[bv["val"][0]], writes=[bv["wout"][0]])
                c.op(c.pool, lambda e, nm=nm, r=r, csl=csl: e.dma_start(
                    out=w_out[nm][128 * r:128 * (r + 1), csl], in_=wbf[:, 0, :]),
                    reads=[bv["wout"][0]], inc=16)
    ws1v = ws1.rearrange("(t p) k -> p t k", p=128)
    ows1v = o_ws1.rearrange("(t p) k -> p t k", p=128)
    c.op(c.qin, lambda e: e.dma_start(
        out=wtile[:, :256].rearrange("p (t k) -> p t k", k=16), in_=ws1v),
        writes=[bv["val"][0]], inc=16)
    c.op(c.dve, lambda e: e.tensor_copy(out=wbf[:, 0, :256], in_=wtile[:, :256]),
         reads=[bv["val"][0]], writes=[bv["wout"][0]])
    c.op(c.pool, lambda e: e.dma_start(
        out=ows1v, in_=wbf[:, 0, :256].rearrange("p (t k) -> p t k", k=16)),
        reads=[bv["wout"][0]], inc=16)
    c.op(c.qin, lambda e: e.dma_start(out=wtile[:16, :], in_=ws2[:]),
         writes=[bv["val"][0]], inc=16)
    c.op(c.dve, lambda e: e.tensor_copy(out=wbf[:16, 0, :], in_=wtile[:16, :]),
         reads=[bv["val"][0]], writes=[bv["wout"][0]])
    c.op(c.pool, lambda e: e.dma_start(out=o_ws2[:], in_=wbf[:16, 0, :]),
         reads=[bv["wout"][0]], inc=16)

    c.finalize_queues()
    return c.build()


def _emit_ln(c, sb, bv, pss, tag):
    x, xln = sb["x"], sb["xln"]
    sq0, sq1 = sb["sq0"], sb["sq1"]
    stat = sb["stat"]
    ones_col, ones_row = sb["ones_col"], sb["ones_row"]
    ps_b, ps_b2 = pss["ps_b"], pss["ps_b2"]
    bx, bxln = bv["x"], bv["xln"]
    bsq = (bv["sq0"], bv["sq1"])
    bstat, bones = bv["stat"], bv["ones"]
    bpb, bpb2 = bv["ps_b"], bv["ps_b2"]
    eps = c.const_cell(EPS, p=1)
    for tt in range(2):
        t0 = 512 * tt
        c.mm_group(bpb, [
            (lambda e, dc=dc, t0=t0: e.matmul(
                ps_b[0:1, :], ones_col[:], x[:, dc, t0:t0 + 512],
                start=(dc == 0), stop=(dc == 15)), [bx, bones])
            for dc in range(16)])
        c.op(c.dve, lambda e, t0=t0: e.tensor_scalar(
            out=stat[0:1, 0, :], in0=ps_b[0:1, :], scalar1=1.0 / D,
            scalar2=None, op0=OP.mult),
            reads=[bpb], writes=[bstat])
        for dc in range(16):
            sq = sq0 if dc % 2 == 0 else sq1
            c.op(c.act, lambda e, dc=dc, t0=t0, sq=sq: e.activation(
                out=sq[:], in_=x[:, dc, t0:t0 + 512], func=AF.Square),
                reads=[bx], writes=[bsq[dc % 2]])
            c.mm_group(bpb2, [
                (lambda e, dc=dc, sq=sq: e.matmul(
                    ps_b2[0:1, :], ones_col[:], sq[:],
                    start=(dc == 0), stop=(dc == 15)), [bsq[dc % 2], bones])
            ], inc=1)
        c.op(c.dve, lambda e: e.tensor_scalar(
            out=stat[0:1, 1, :], in0=ps_b2[0:1, :], scalar1=1.0 / D,
            scalar2=None, op0=OP.mult),
            reads=[bpb2], writes=[bstat])
        c.op(c.dve, lambda e: e.tensor_tensor(
            out=stat[0:1, 2, :], in0=stat[0:1, 0, :], in1=stat[0:1, 0, :],
            op=OP.mult), reads=[bstat], writes=[bstat])
        c.op(c.dve, lambda e: e.tensor_tensor(
            out=stat[0:1, 1, :], in0=stat[0:1, 1, :], in1=stat[0:1, 2, :],
            op=OP.subtract), reads=[bstat], writes=[bstat])
        c.op(c.act, lambda e: e.activation(
            out=stat[0:1, 2, :], in_=stat[0:1, 1, :], func=AF.Ln,
            scale=1.0, bias=eps), reads=[bstat, c.const_buf], writes=[bstat])
        c.op(c.act, lambda e: e.activation(
            out=stat[0:1, 3, :], in_=stat[0:1, 2, :], func=AF.Exp,
            scale=-0.5), reads=[bstat], writes=[bstat])
        c.op(c.dve, lambda e: e.tensor_tensor(
            out=stat[0:1, 4, :], in0=stat[0:1, 0, :], in1=stat[0:1, 3, :],
            op=OP.mult), reads=[bstat], writes=[bstat])
        c.mm_group(bpb, [(lambda e: e.matmul(
            ps_b[:, :], ones_row[:], stat[0:1, 3, :], start=True, stop=True),
            [bstat, bones])])
        c.mm_group(bpb2, [(lambda e: e.matmul(
            ps_b2[:, :], ones_row[:], stat[0:1, 4, :], start=True, stop=True),
            [bstat, bones])])
        for dc in range(16):
            sq = sq0 if dc % 2 == 0 else sq1
            c.op(c.dve, lambda e, dc=dc, t0=t0, sq=sq: e.tensor_tensor(
                out=sq[:], in0=x[:, dc, t0:t0 + 512], in1=ps_b[:, :],
                op=OP.mult), reads=[bx, bpb], writes=[bsq[dc % 2]])
            c.op(c.dve, lambda e, dc=dc, t0=t0, sq=sq: e.tensor_tensor(
                out=xln[:, dc, t0:t0 + 512], in0=sq[:], in1=ps_b2[:, :],
                op=OP.subtract), reads=[bsq[dc % 2], bpb2], writes=[bxln])


def _emit_ffn(c, sb, bv, pss, wgt, wut, wdt):
    x, xln = sb["x"], sb["xln"]
    wa, gate_g, inter = sb["wa"], sb["gate_g"], sb["inter"]
    bx, bxln = bv["x"], bv["xln"]
    bwa = (bv["wa0"], bv["wa1"])
    bgg, bint = bv["gate_g"], bv["inter"]
    ps_g = (pss["ps_g0"], pss["ps_g1"])
    bps_g = (bv["ps_g0"], bv["ps_g1"])
    ps_u = (pss["ps_u0"], pss["ps_u1"])
    bps_u = (bv["ps_u0"], bv["ps_u1"])
    ps_d = (pss["ps_d0"], pss["ps_d1"])
    bps_d = (bv["ps_d0"], bv["ps_d1"])
    wgv = wgt.rearrange("(kc p) n -> p kc n", p=128)
    wuv = wut.rearrange("(kc p) n -> p kc n", p=128)
    wdv = wdt.rearrange("(nc p) d -> p nc d", p=128)

    slot = [0]

    def load(view, cols, nrows=16, row0=0):
        s = slot[0]
        slot[0] = 1 - s
        c.op(c.qin, lambda e: e.dma_start(
            out=wa[:, s, 0:nrows, :], in_=view[:, row0:row0 + nrows, cols]),
            writes=[bwa[s]], inc=16)
        return s

    for g in range(16):
        for w2 in range(2):
            n0 = 1024 * g + 512 * w2
            sg = load(wgv, slice(n0, n0 + 512))
            su = load(wuv, slice(n0, n0 + 512))
            for nc4 in range(4):
                ic = 4 * w2 + nc4
                for tt in range(2):
                    t0 = 512 * tt
                    i = tt
                    c.mm_group(bps_g[i], [
                        (lambda e, kc=kc, nc4=nc4, t0=t0, i=i, sg=sg: e.matmul(
                            ps_g[i][:], wa[:, sg, kc, 128 * nc4:128 * (nc4 + 1)],
                            xln[:, kc, t0:t0 + 512],
                            start=(kc == 0), stop=(kc == 15)),
                         [bwa[sg], bxln]) for kc in range(16)])
                    c.op(c.act, lambda e, nc4=nc4, t0=t0, i=i: e.activation(
                        out=gate_g[:, nc4, t0:t0 + 512], in_=ps_g[i][:],
                        func=AF.Gelu_apprx_tanh),
                        reads=[bps_g[i]], writes=[bgg])
                for tt in range(2):
                    t0 = 512 * tt
                    i = tt
                    c.mm_group(bps_u[i], [
                        (lambda e, kc=kc, nc4=nc4, t0=t0, i=i, su=su: e.matmul(
                            ps_u[i][:], wa[:, su, kc, 128 * nc4:128 * (nc4 + 1)],
                            xln[:, kc, t0:t0 + 512],
                            start=(kc == 0), stop=(kc == 15)),
                         [bwa[su], bxln]) for kc in range(16)])
                    c.op(c.dve, lambda e, ic=ic, nc4=nc4, t0=t0, i=i: e.tensor_tensor(
                        out=inter[:, ic, t0:t0 + 512], in0=gate_g[:, nc4, t0:t0 + 512],
                        in1=ps_u[i][:], op=OP.mult),
                        reads=[bgg, bps_u[i]], writes=[bint])
        for dwin in range(4):
            sd = load(wdv, slice(512 * dwin, 512 * (dwin + 1)), nrows=8, row0=8 * g)
            for dc4 in range(4):
                dchunk = 4 * dwin + dc4
                for tt in range(2):
                    t0 = 512 * tt
                    i = tt
                    c.mm_group(bps_d[i], [
                        (lambda e, ncj=ncj, dc4=dc4, t0=t0, i=i, sd=sd: e.matmul(
                            ps_d[i][:], wa[:, sd, ncj, 128 * dc4:128 * (dc4 + 1)],
                            inter[:, ncj, t0:t0 + 512],
                            start=(ncj == 0), stop=(ncj == 7)),
                         [bwa[sd], bint]) for ncj in range(8)])
                    c.op(c.dve, lambda e, dchunk=dchunk, t0=t0, i=i: e.tensor_tensor(
                        out=x[:, dchunk, t0:t0 + 512], in0=x[:, dchunk, t0:t0 + 512],
                        in1=ps_d[i][:], op=OP.add),
                        reads=[bps_d[i], bx], writes=[bx])


def build_ffn_launch(is_l3):
    c = Ctx("l3" if is_l3 else "l1")
    xin = c.dram_in("xin", [D, TC], F32)
    wgt = c.dram_in("wgt", [D, MLP], BF16)
    wut = c.dram_in("wut", [D, MLP], BF16)
    wdt = c.dram_in("wdt", [MLP, D], BF16)
    if is_l3:
        og_in = c.dram_in("og", [D, TC], BF16)
        wot_in = c.dram_in("wot", [D, D], BF16)
        xout = c.dram_out("xout", [D, TC], F32)
    else:
        xout = c.dram_out("xout", [D, TC], F32)
        xlnout = c.dram_out("xlnout", [D, TC], BF16)

    sb = {}
    sb["x"] = c.sbuf("x", [128, 16, TC], F32)
    sb["xln"] = c.sbuf("xln", [128, 16, TC], BF16)
    sb["wa"] = c.sbuf("wa", [128, 2, 16, 512], BF16)
    sb["gate_g"] = c.sbuf("gate_g", [128, 4, TC], BF16)
    sb["inter"] = c.sbuf("inter", [128, 8, TC], BF16)
    sb["sq0"] = c.sbuf("sq0", [128, 512], F32)
    sb["sq1"] = c.sbuf("sq1", [128, 512], F32)
    sb["stat"] = c.sbuf("stat", [1, 8, 512], F32)
    sb["ones_col"] = c.sbuf("ones_col", [128, 1], F32)
    sb["ones_row"] = c.sbuf("ones_row", [1, 128], F32)
    if is_l3:
        sb["og"] = c.sbuf("ogsb", [128, 16, TC], BF16)

    pss = {}
    for nm, shp in (("ps_b", [128, 512]), ("ps_b2", [128, 512]),
                    ("ps_g0", [128, 512]), ("ps_g1", [128, 512]),
                    ("ps_u0", [128, 512]), ("ps_u1", [128, 512]),
                    ("ps_d0", [128, 512]), ("ps_d1", [128, 512])):
        pss[nm] = c.psum(shp, F32, name=nm)

    bv = {nm: Buf(nm) for nm in
          ("x", "xln", "wa0", "wa1", "gate_g", "inter", "sq0", "sq1",
           "stat", "ones", "og",
           "ps_b", "ps_b2", "ps_g0", "ps_g1", "ps_u0", "ps_u1",
           "ps_d0", "ps_d1")}

    c.op(c.dve, lambda e: e.memset(sb["ones_col"][:], 1.0), writes=[bv["ones"]])
    c.op(c.dve, lambda e: e.memset(sb["ones_row"][:], 1.0), writes=[bv["ones"]])
    c.op(c.qin, lambda e: e.dma_start(
        out=sb["x"][:], in_=xin.rearrange("(dc p) t -> p dc t", p=128)),
        writes=[bv["x"]], inc=16)

    if is_l3:
        c.op(c.qin, lambda e: e.dma_start(
            out=sb["og"][:], in_=og_in.rearrange("(hc p) t -> p hc t", p=128)),
            writes=[bv["og"]], inc=16)
        wov = wot_in.rearrange("(hc p) d -> p hc d", p=128)
        wslot = [0]
        for dwin in range(4):
            s = wslot[0]
            wslot[0] = 1 - s
            c.op(c.qin, lambda e, dwin=dwin, s=s: e.dma_start(
                out=sb["wa"][:, s, :, :],
                in_=wov[:, :, 512 * dwin:512 * (dwin + 1)]),
                writes=[bv["wa%d" % s]], inc=16)
            for dc4 in range(4):
                dchunk = 4 * dwin + dc4
                for tt in range(2):
                    t0 = 512 * tt
                    i = (dchunk * 2 + tt) % 2
                    psd = pss["ps_d%d" % i]
                    c.mm_group(bv["ps_d%d" % i], [
                        (lambda e, hc=hc, dc4=dc4, t0=t0, psd=psd, s=s: e.matmul(
                            psd[:], sb["wa"][:, s, hc, 128 * dc4:128 * (dc4 + 1)],
                            sb["og"][:, hc, t0:t0 + 512],
                            start=(hc == 0), stop=(hc == 15)),
                         [bv["wa%d" % s], bv["og"]]) for hc in range(16)])
                    c.op(c.dve, lambda e, dchunk=dchunk, t0=t0, i=i: e.tensor_tensor(
                        out=sb["x"][:, dchunk, t0:t0 + 512],
                        in0=sb["x"][:, dchunk, t0:t0 + 512],
                        in1=pss["ps_d%d" % i][:], op=OP.add),
                        reads=[bv["ps_d%d" % i], bv["x"]], writes=[bv["x"]])

    _emit_ln(c, sb, bv, pss, "ln")
    _emit_ffn(c, sb, bv, pss, wgt, wut, wdt)

    if not is_l3:
        _emit_ln(c, sb, bv, pss, "ln2")
        c.op(c.pool, lambda e: e.dma_start(
            out=xlnout.rearrange("(dc p) t -> p dc t", p=128), in_=sb["xln"][:]),
            reads=[bv["xln"]], inc=16)
    c.op(c.pool, lambda e: e.dma_start(
        out=xout.rearrange("(dc p) t -> p dc t", p=128), in_=sb["x"][:]),
        reads=[bv["x"]], inc=16)
    c.finalize_queues()
    return c.build()



def build_l2(debug=False, phases="all", nsu=4, stmask="012345", ilv=True, ilv_n=99):
    assert not debug, "debug hooks removed in the interleaved version"
    c = Ctx("l2")
    xall = c.dram_in("xall", [D, B * T], BF16)
    wq = c.dram_in("wq", [D, 256], BF16)
    wk = c.dram_in("wk", [D, 256], BF16)
    wv = c.dram_in("wv", [D, 256], BF16)
    wga = c.dram_in("wga", [D, 256], BF16)
    ws1 = c.dram_in("ws1", [D, 16], BF16)
    ws2 = c.dram_in("ws2", [16, 128], BF16)
    bs2A = c.dram_in("bs2A", [64, 1], F32)
    bs2B = c.dram_in("bs2B", [64, 1], F32)
    bs2r = c.dram_in("bs2r", [1, 128], F32)
    gnwr = c.dram_in("gnwr", [1, 128], F32)
    cosd = c.dram_in("cosd", [128, T], BF16)
    sind = c.dram_in("sind", [128, T], BF16)
    triu = c.dram_in("triu", [128, 128], BF16)
    idn = c.dram_in("idn", [128, 128], BF16)
    ogo = c.dram_out("ogo", [256, B * T], BF16)

    xw = c.sbuf("xw", [128, 16, 512], BF16)
    wpr = c.sbuf("wpr", [128, 4, 16, 256], BF16)
    ws1s = c.sbuf("ws1s", [128, 16, 16], BF16)
    ws2s = c.sbuf("ws2s", [16, 128], BF16)
    bs2rs = c.sbuf("bs2rs", [1, 128], BF16)
    qkvs = [c.sbuf("qkvs0", [128, 4, 2048 * 2], BF16),
            c.sbuf("qkvs1", [128, 4, 2048 * 2], BF16)]
    rpb1 = c.sbuf("rpb1", [128, 2048], BF16)
    rpb2 = c.sbuf("rpb2", [128, 2048], BF16)
    s1a = [c.sbuf("s1a0", [16, 2048], BF16), c.sbuf("s1a1", [16, 2048], BF16)]
    esmm = [[c.sbuf("esmm0A", [64, 2048], BF16), c.sbuf("esmm0B", [64, 2048], BF16)],
            [c.sbuf("esmm1A", [64, 2048], BF16), c.sbuf("esmm1B", [64, 2048], BF16)]]
    estm = [c.sbuf("estm0", [128, 16, 128], BF16),
            c.sbuf("estm1", [128, 16, 128], BF16)]
    csb = c.sbuf("csb", [128, T], BF16)
    snb = c.sbuf("snb", [128, T], BF16)
    trs = c.sbuf("trs", [128, 128], BF16)
    ids = c.sbuf("ids", [128, 128], BF16)
    bsA = c.sbuf("bsA", [64, 1], F32)
    bsB = c.sbuf("bsB", [64, 1], F32)
    ones_col = c.sbuf("onc", [128, 1], F32)
    ones_row = c.sbuf("onr", [1, 128], F32)
    onesb_row = c.sbuf("onbr", [1, 128], BF16)
    onesb_col = c.sbuf("onbc", [128, 1], BF16)
    zcol = c.sbuf("zcol", [1, 128], BF16)
    zrow = c.sbuf("zrow", [1, 512], BF16)
    ogsb = c.sbuf("ogsb2", [128, 2, 2048], BF16)
    kts = [c.sbuf("ktsA", [128, 128], BF16), c.sbuf("ktsB", [128, 128], BF16)]
    vts = [c.sbuf("vtsA", [128, 128], BF16), c.sbuf("vtsB", [128, 128], BF16)]
    ats = [c.sbuf("atsA", [128, 128], BF16), c.sbuf("atsB", [128, 128], BF16)]
    w2s = [c.sbuf("w2sA", [128, 128], BF16), c.sbuf("w2sB", [128, 128], BF16)]
    qeps = [c.sbuf("qepsA", [128, 128], BF16), c.sbuf("qepsB", [128, 128], BF16)]
    qets = [c.sbuf("qetsA", [64, 128], BF16), c.sbuf("qetsB", [64, 128], BF16)]
    enzc = c.sbuf("enzc", [128, 128], F32)
    okf = [c.sbuf("okfA", [128, 64], F32), c.sbuf("okfB", [128, 64], F32)]
    exf = [c.sbuf("exfA", [128, 64], F32), c.sbuf("exfB", [128, 64], F32)]
    mxs = [c.sbuf("mxsA", [128, 2], F32), c.sbuf("mxsB", [128, 2], F32)]
    s1sb = [c.sbuf("s1sbA", [128, 64], BF16), c.sbuf("s1sbB", [128, 64], BF16)]
    s2sb = [c.sbuf("s2sbA", [64, 128], BF16), c.sbuf("s2sbB", [64, 128], BF16)]
    c1f = c.sbuf("c1f", [1, 128], F32)
    c1b = c.sbuf("c1b", [1, 128], BF16)
    oraw = [c.sbuf("orawA", [128, 2048], BF16), c.sbuf("orawB", [128, 2048], BF16)]
    rnb = [c.sbuf("rnbA", [1, 2048], BF16), c.sbuf("rnbB", [1, 2048], BF16)]
    rbw = c.sbuf("rbw", [128, 512], F32)
    onw = c.sbuf("onw", [128, 512], F32)
    gnwrb = c.sbuf("gnwrb", [1, 128], BF16)

    ps_states = c.psum([128, 512], F32, name="ps_states")
    ps_f1 = [c.psum([128, 512], F32, name="ps_f1a"),
             c.psum([128, 512], F32, name="ps_f1b")]
    ps_f2 = [c.psum([128, 512], F32, name="ps_f2a"),
             c.psum([128, 512], F32, name="ps_f2b")]
    ps_bf = c.psum([128, 1024], BF16, name="ps_bf")
    ps_in = [c.psum([128, 512], F32, name="ps_in0"),
             c.psum([128, 512], F32, name="ps_in1")]

    bv = {}
    for nm in ("xw", "wpr", "ws1s", "ws2s", "bs2rs", "rpb1",
               "rpb2", "rope", "trs", "ids", "bsA", "bsB",
               "ones", "ogsb", "ps_states", "rbw", "onw", "gnws"):
        bv[nm] = Buf(nm)
    for nm in ("qkvs", "s1a", "estm",
               "kts", "vts", "ats", "w2s", "qeps", "qets",
               "okf", "exf", "mxs", "s1sb", "s2sb",
               "oraw", "rnb",
               "ps_f1", "ps_f2", "ps_bf",
               "r_at", "r_w2", "r_o", "r_ok", "r_osum",
               "r_rb", "r_kt", "r_vt", "r_qet"):
        bv[nm] = [Buf(nm + "0"), Buf(nm + "1")]
    for nm in ("enzc", "c1f", "c1b", "r_cum", "r_col"):
        bv[nm] = Buf(nm)
    bv["esmm"] = [[Buf("esmm0A"), Buf("esmm0B")], [Buf("esmm1A"), Buf("esmm1B")]]
    bv["ps_in"] = [Buf("ps_in0"), Buf("ps_in1")]
    bank_f1 = [Bank(), Bank()]
    bank_f2 = [Bank(), Bank()]
    bank_bf = Bank()
    bank_in = [Bank(), Bank()]
    bank_states = Bank()
    bv["ps_states"].bank = bank_states
    bv["ps_in"][0].bank = bank_in[0]
    bv["ps_in"][1].bank = bank_in[1]
    for hb in range(2):
        for nm in ("ps_f1", "r_at", "r_w2", "r_o"):
            bv[nm][hb].bank = bank_f1[hb]
        for nm in ("ps_f2", "r_ok", "r_osum", "r_rb"):
            bv[nm][hb].bank = bank_f2[hb]
        for nm in ("ps_bf", "r_kt", "r_vt", "r_qet"):
            bv[nm][hb].bank = bank_bf
    bv["r_cum"].bank = bank_f2[0]
    bv["r_col"].bank = bank_f2[0]
    bv["r_ok"][0].bank = bank_f2[0]
    bv["r_ok"][1].bank = bank_f2[1]

    def f1_all(hb):
        return [bv["ps_f1"][hb], bv["r_at"][hb], bv["r_w2"][hb], bv["r_o"][hb]]

    def f2_all(hb):
        out = [bv["ps_f2"][hb], bv["r_ok"][hb], bv["r_osum"][hb], bv["r_rb"][hb]]
        if hb == 0:
            out += [bv["r_cum"], bv["r_col"]]
        return out

    c.op(c.dve, lambda e: e.memset(ones_col[:], 1.0), writes=[bv["ones"]])
    c.op(c.dve, lambda e: e.memset(ones_row[:], 1.0), writes=[bv["ones"]])
    c.op(c.dve, lambda e: e.memset(onesb_row[:], 1.0), writes=[bv["ones"]])
    c.op(c.dve, lambda e: e.memset(onesb_col[:], 1.0), writes=[bv["ones"]])
    c.op(c.dve, lambda e: e.memset(zcol[:], 0.0), writes=[bv["ones"]])
    c.op(c.dve, lambda e: e.memset(zrow[:], 0.0), writes=[bv["ones"]])
    for hb in range(2):
        c.op(c.dve, lambda e, hb=hb: e.memset(qeps[hb][:], 0.0),
             writes=[bv["qeps"][hb]])
    for nm, src, dst in (("trs", triu, trs), ("ids", idn, ids),
                         ("rope", cosd, csb), ("rope", sind, snb),
                         ("ws2s", ws2, ws2s)):
        c.op(c.qin, lambda e, src=src, dst=dst: e.dma_start(out=dst[:], in_=src[:]),
             writes=[bv[nm]], inc=16)
    c.op(c.pool, lambda e: e.dma_start(out=bs2rs[:], in_=bs2r[:]),
         writes=[bv["bs2rs"]], inc=16)
    c.op(c.pool, lambda e: e.dma_start(out=gnwrb[:], in_=gnwr[:]),
         writes=[bv["gnws"]], inc=16)
    c.op(c.qin, lambda e: e.dma_start(out=bsA[:], in_=bs2A[:]), writes=[bv["bsA"]], inc=16)
    c.op(c.qin, lambda e: e.dma_start(out=bsB[:], in_=bs2B[:]), writes=[bv["bsB"]], inc=16)
    for i, w in enumerate((wq, wk, wv, wga)):
        c.op(c.qin, lambda e, i=i, w=w: e.dma_start(
            out=wpr[:, i, :, :], in_=w.rearrange("(kc p) h -> p kc h", p=128)),
            writes=[bv["wpr"]], inc=16)
    c.op(c.qin, lambda e: e.dma_start(
        out=ws1s[:], in_=ws1.rearrange("(kc p) r -> p kc r", p=128)),
        writes=[bv["ws1s"]], inc=16)

    K_Q, K_K, K_V, K_S = 0, 1, 2, 3
    xv = xall.rearrange("(kc p) t -> p kc t", p=128)
    eps_c = c.const_cell(EPS, p=1)

    def hslice(head, sl):
        return slice(2048 * head + sl.start, 2048 * head + sl.stop)

    def unit_items(b, h2, bs):
        items = []
        tbase = b * T + h2 * 2048
        full = slice(0, 2048)

        for tw in range(4):
            t0 = tbase + 512 * tw
            dst = slice(512 * tw, 512 * (tw + 1))

            def load_xw(t0=t0):
                c.op(c.qin, lambda e: e.dma_start(
                    out=xw[:], in_=xv[:, :, t0:t0 + 512]),
                    writes=[bv["xw"]], inc=16)
            items.append(load_xw)
            for head in range(2):
                for pj in (K_Q, K_K, K_V, K_S):
                    def proj(head=head, pj=pj, dst=dst, bs=bs):
                        i = pj % 2
                        psb = ps_in[i]
                        pbv = [bv["ps_in"][i]]
                        hoff = 128 * head
                        c.mm_group(pbv, [
                            (lambda e, kc=kc, pj=pj, hoff=hoff, psb=psb: e.matmul(
                                psb[:], wpr[:, pj, kc, hoff:hoff + 128],
                                xw[:, kc, :], start=(kc == 0), stop=(kc == 15),
                                skip_group_check=True),
                             [bv["wpr"], bv["xw"]]) for kc in range(16)])
                        fn = AF.Silu if pj == K_S else AF.Identity
                        c.op(c.act, lambda e, fn=fn: e.activation(
                            out=qkvs[bs][:, pj, hslice(head, dst)],
                            in_=psb[:], func=fn),
                            reads=pbv, writes=[bv["qkvs"][bs]])
                    items.append(proj)

            def s1proj(dst=dst, bs=bs):
                c.mm_group([bv["ps_in"][0]], [
                    (lambda e, kc=kc: e.matmul(
                        ps_in[0][0:16, 0:512], ws1s[:, kc, :], xw[:, kc, :],
                        start=(kc == 0), stop=(kc == 15), skip_group_check=True),
                     [bv["ws1s"], bv["xw"]]) for kc in range(16)])
                c.op(c.dve, lambda e: e.tensor_copy(
                    out=s1a[bs][:, dst], in_=ps_in[0][0:16, 0:512]),
                    reads=[bv["ps_in"][0]], writes=[bv["s1a"][bs]])
            items.append(s1proj)

        rope_pos = slice(h2 * 2048, h2 * 2048 + 2048)
        for head in range(2):
            for pj in (K_Q, K_K):
                def rope(head=head, pj=pj, bs=bs, rope_pos=rope_pos):
                    src = qkvs[bs][:, pj, hslice(head, slice(0, 2048))]
                    c.op(c.pool, lambda e: e.dma_start(
                        out=rpb2[0:64, :], in_=src[64:128, :]),
                        reads=[bv["qkvs"][bs]], writes=[bv["rpb2"]], inc=16)
                    c.op(c.pool, lambda e: e.dma_start(
                        out=rpb2[64:128, :], in_=src[0:64, :]),
                        reads=[bv["qkvs"][bs]], writes=[bv["rpb2"]], inc=16)
                    c.op(c.dve, lambda e: e.tensor_tensor(
                        out=rpb1[:], in0=src, in1=csb[:, rope_pos], op=OP.mult),
                        reads=[bv["qkvs"][bs], bv["rope"]], writes=[bv["rpb1"]])
                    c.op(c.dve, lambda e: e.tensor_tensor(
                        out=rpb2[:], in0=rpb2[:], in1=snb[:, rope_pos], op=OP.mult),
                        reads=[bv["rpb2"], bv["rope"]], writes=[bv["rpb2"]])
                    c.op(c.dve, lambda e: e.tensor_tensor(
                        out=src, in0=rpb1[:], in1=rpb2[:], op=OP.subtract),
                        reads=[bv["rpb1"], bv["rpb2"]], writes=[bv["qkvs"][bs]])
                    if pj == K_Q:
                        c.op(c.dve, lambda e: e.tensor_scalar(
                            out=src, in0=src, scalar1=SCALE, scalar2=None,
                            op0=OP.mult),
                            reads=[bv["qkvs"][bs]], writes=[bv["qkvs"][bs]])
                items.append(rope)

        for head in range(2):
            for tw in range(4):
                def esm(head=head, tw=tw, bs=bs):
                    bsx = bsA if head == 0 else bsB
                    bsxb = bv["bsA"] if head == 0 else bv["bsB"]
                    dst = slice(512 * tw, 512 * (tw + 1))
                    c.mm_group([bv["ps_in"][head]], [(lambda e: e.matmul(
                        ps_in[head][0:64, 0:512],
                        ws2s[:, 64 * head:64 * (head + 1)],
                        s1a[bs][:, dst],
                        start=True, stop=True, skip_group_check=True),
                        [bv["ws2s"], bv["s1a"][bs]])])
                    c.op(c.dve, lambda e: e.tensor_scalar(
                        out=onw[0:64, 0:512], in0=ps_in[head][0:64, 0:512],
                        scalar1=bsx[:], scalar2=-CLAMP, op0=OP.add, op1=OP.max),
                        reads=[bv["ps_in"][head], bsxb], writes=[bv["onw"]])
                    c.op(c.dve, lambda e: e.tensor_scalar(
                        out=onw[0:64, 0:512], in0=onw[0:64, 0:512],
                        scalar1=CLAMP, scalar2=None, op0=OP.min),
                        reads=[bv["onw"]], writes=[bv["onw"]])
                    c.op(c.act, lambda e: e.activation(
                        out=esmm[bs][head][:, dst], in_=onw[0:64, 0:512],
                        func=AF.Exp),
                        reads=[bv["onw"]], writes=[bv["esmm"][bs][head]])
                items.append(esm)
        for tc in range(16):
            def est(tc=tc, bs=bs):
                i = tc % 2
                c.mm_group([bv["ps_in"][i]], [
                    (lambda e: e.matmul(
                        ps_in[i][:, 0:128], s1a[bs][:, 128 * tc:128 * (tc + 1)],
                        ws2s[:], start=True, stop=False, skip_group_check=True),
                     [bv["s1a"][bs], bv["ws2s"]]),
                    (lambda e: e.matmul(
                        ps_in[i][:, 0:128], onesb_row[:], bs2rs[:],
                        start=False, stop=True, skip_group_check=True),
                     [bv["ones"], bv["bs2rs"]]),
                ])
                c.op(c.dve, lambda e, i=i: e.tensor_scalar(
                    out=rbw[:, 0:128], in0=ps_in[i][:, 0:128],
                    scalar1=-CLAMP, scalar2=CLAMP, op0=OP.max, op1=OP.min),
                    reads=[bv["ps_in"][i]], writes=[bv["rbw"]])
                c.op(c.act, lambda e: e.activation(
                    out=estm[bs][:, tc, :], in_=rbw[:, 0:128], func=AF.Exp),
                    reads=[bv["rbw"]], writes=[bv["estm"][bs]])
            items.append(est)
        return items

    def chunk_stages(head, tc, first, last, bs):
        hb = head
        s1col = slice(64 * head, 64 * (head + 1))
        s2col = slice(128 + 128 * head, 256 + 128 * head)
        t128 = slice(128 * tc, 128 * (tc + 1))
        es_m = slice(64 * head, 64 * (head + 1))
        q_ap = qkvs[bs][:, K_Q, hslice(head, t128)]
        k_ap = qkvs[bs][:, K_K, hslice(head, t128)]
        v_ap = qkvs[bs][:, K_V, hslice(head, t128)]
        bqk = bv["qkvs"][bs]

        bo = 384 * hb

        def st0():
            c.mm_group(bv["r_kt"][hb], [(lambda e: e.transpose(
                ps_bf[:, bo + 0:bo + 128], k_ap, ids[:]), [bqk, bv["ids"]])])
            c.op(c.act, lambda e: e.activation(out=kts[hb][:], in_=ps_bf[:, bo + 0:bo + 128], func=AF.Identity),
                 reads=[bv["r_kt"][hb]], writes=[bv["kts"][hb]], selfw=False)
            c.mm_group(bv["r_vt"][hb], [(lambda e: e.transpose(
                ps_bf[:, bo + 128:bo + 256], v_ap, ids[:]), [bqk, bv["ids"]])])
            c.op(c.act, lambda e: e.activation(out=vts[hb][:], in_=ps_bf[:, bo + 128:bo + 256], func=AF.Identity),
                 reads=[bv["r_vt"][hb]], writes=[bv["vts"][hb]], selfw=False)
            if not first:
                c.op(c.act, lambda e: e.activation(
                    out=s1sb[hb][:], in_=ps_states[:, s1col], func=AF.Identity),
                    reads=[bv["ps_states"]], writes=[bv["s1sb"][hb]], selfw=False)
                c.op(c.act, lambda e: e.activation(
                    out=s2sb[hb][:], in_=ps_states[0:64, s2col], func=AF.Identity),
                    reads=[bv["ps_states"]], writes=[bv["s2sb"][hb]], selfw=False)

        def st1():
            c.mm_group(bv["r_at"][hb], [(lambda e: e.matmul(
                ps_f1[hb][:, 0:128], k_ap, q_ap, start=True, stop=True,
                skip_group_check=True), [bqk])])
            c.op(c.dve, lambda e: e.tensor_tensor(
                out=ats[hb][:], in0=trs[:], in1=ps_f1[hb][:, 0:128], op=OP.mult),
                reads=[bv["trs"], bv["r_at"][hb]], writes=[bv["ats"][hb]], selfw=False)
            if hb == 0:
                c.mm_group(bv["r_cum"], [
                    (lambda e: e.matmul(ps_f2[0][:, 0:128], trs[:],
                                        estm[bs][:, tc, :], start=True,
                                        stop=False, skip_group_check=True),
                     [bv["trs"], bv["estm"][bs]]),
                    (lambda e: e.matmul(ps_f2[0][:, 0:128], onesb_row[:], c1b[:],
                                        start=False, stop=True,
                                        skip_group_check=True),
                     [bv["ones"], bv["c1b"]]),
                ])
                c.op(c.dve, lambda e: e.reciprocal(out=enzc[:], in_=ps_f2[0][:, 0:128]),
                     reads=[bv["r_cum"]], writes=[bv["enzc"]], selfw=False)
                c.mm_group(bv["r_col"], [(lambda e: e.matmul(
                    ps_f2[0][0:1, 128:256], onesb_col[:], estm[bs][:, tc, :],
                    start=True, stop=True, skip_group_check=True),
                    [bv["ones"], bv["estm"][bs]])])
                c.op(c.dve, lambda e: e.tensor_tensor(
                    out=c1f[:], in0=c1f[:], in1=ps_f2[0][0:1, 128:256],
                    op=OP.add),
                    reads=[bv["r_col"], bv["c1f"]], writes=[bv["c1f"]], selfw=False)
                c.op(c.act, lambda e: e.activation(out=c1b[:], in_=c1f[:],
                     func=AF.Identity),
                     reads=[bv["c1f"]], writes=[bv["c1b"]], selfw=False)

        ok_reg = (ps_f2[0], slice(256, 320)) if hb == 0 else (ps_f2[1], slice(0, 64))

        def st2():
            okp, oks = ok_reg
            parts = []
            if not first:
                parts.append((lambda e: e.matmul(
                    okp[:, oks], q_ap, s1sb[hb][:], start=True, stop=False,
                    skip_group_check=True), [bqk, bv["s1sb"][hb]]))
            parts.append((lambda e, st=first: e.matmul(
                okp[:, oks], ats[hb][:], estm[bs][:, tc, es_m],
                start=st, stop=True, skip_group_check=True),
                [bv["ats"][hb], bv["estm"][bs]]))
            c.mm_group(bv["r_ok"][hb], parts)
            c.op(c.dve, lambda e: e.tensor_tensor(
                out=okf[hb][:], in0=enzc[:, es_m], in1=okp[:, oks], op=OP.mult),
                reads=[bv["enzc"], bv["r_ok"][hb]], writes=[bv["okf"][hb]], selfw=False)
            c.op(c.dve, lambda e: e.reduce_max(
                out=mxs[hb][:, 0:1], in_=okf[hb][:], axis=AX, negate=True),
                reads=[bv["okf"][hb]], writes=[bv["mxs"][hb]], selfw=False)
            c.op(c.act, lambda e: e.activation(
                out=exf[hb][:], in_=okf[hb][:], func=AF.Exp, bias=mxs[hb][:, 0:1]),
                reads=[bv["okf"][hb], bv["mxs"][hb]], writes=[bv["exf"][hb]])
            c.op(c.dve, lambda e: e.reduce_sum(
                out=mxs[hb][:, 1:2], in_=exf[hb][:], axis=AX),
                reads=[bv["exf"][hb]], writes=[bv["mxs"][hb]], selfw=False)
            c.op(c.dve, lambda e: e.reciprocal(out=mxs[hb][:, 0:1], in_=mxs[hb][:, 1:2]),
                 reads=[bv["mxs"][hb]], writes=[bv["mxs"][hb]])
            c.op(c.dve, lambda e: e.tensor_scalar(
                out=exf[hb][:], in0=exf[hb][:], scalar1=mxs[hb][:, 0:1],
                scalar2=None, op0=OP.mult),
                reads=[bv["mxs"][hb], bv["exf"][hb]], writes=[bv["exf"][hb]])
            c.op(c.dve, lambda e: e.tensor_tensor(
                out=qeps[hb][:, 0:64], in0=exf[hb][:], in1=enzc[:, es_m],
                op=OP.mult),
                reads=[bv["exf"][hb], bv["enzc"]], writes=[bv["qeps"][hb]], selfw=False)

        def st3():
            c.mm_group(bv["r_qet"][hb], [(lambda e: e.transpose(
                ps_bf[:, bo + 256:bo + 384], qeps[hb][:], ids[:]),
                [bv["qeps"][hb], bv["ids"]])])
            c.op(c.act, lambda e: e.activation(
                out=qets[hb][:], in_=ps_bf[0:64, bo + 256:bo + 384], func=AF.Identity),
                reads=[bv["r_qet"][hb]], writes=[bv["qets"][hb]], selfw=False)
            c.mm_group(bv["r_w2"][hb], [(lambda e: e.matmul(
                ps_f1[hb][:, 128:256], esmm[bs][head][:, t128], qets[hb][:],
                start=True, stop=True, skip_group_check=True),
                [bv["esmm"][bs][head], bv["qets"][hb]])])
            c.op(c.dve, lambda e: e.tensor_tensor(
                out=w2s[hb][:], in0=trs[:], in1=ps_f1[hb][:, 128:256], op=OP.mult),
                reads=[bv["trs"], bv["r_w2"][hb]], writes=[bv["w2s"][hb]], selfw=False)

        def st4():
            parts = []
            if not first:
                parts.append((lambda e: e.matmul(
                    ps_f1[hb][:, 256:384], s2sb[hb][:], qets[hb][:],
                    start=True, stop=False, skip_group_check=True),
                    [bv["s2sb"][hb], bv["qets"][hb]]))
            parts.append((lambda e, st=first: e.matmul(
                ps_f1[hb][:, 256:384], vts[hb][:], w2s[hb][:], start=st, stop=True,
                skip_group_check=True), [bv["vts"][hb], bv["w2s"][hb]]))
            c.mm_group(bv["r_o"][hb], parts)
            c.mm_group(bv["ps_states"], [(lambda e: e.matmul(
                ps_states[:, s1col], kts[hb][:], estm[bs][:, tc, es_m],
                start=False, stop=last, skip_group_check=True),
                [bv["kts"][hb], bv["estm"][bs]])])
            c.mm_group(bv["ps_states"], [(lambda e: e.matmul(
                ps_states[0:64, s2col], estm[bs][:, tc, es_m], vts[hb][:],
                start=False, stop=last, skip_group_check=True),
                [bv["estm"][bs], bv["vts"][hb]])])

        def st5():
            c.op(c.act, lambda e: e.activation(
                out=oraw[hb][:, t128], in_=ps_f1[hb][:, 256:384], func=AF.Identity),
                reads=[bv["r_o"][hb]], writes=[bv["oraw"][hb]], selfw=False)

        return [st0, st1, st2, st3, st4, st5]

    units = [(b, h2) for b in range(B) for h2 in range(2)][:nsu]
    pending = unit_items(*units[0], 0)
    for it in pending:
        it()
    pending = []
    for ui, (b, h2) in enumerate(units):
        bs = ui % 2
        tbase = b * T + h2 * 2048
        if h2 == 0:
            c.op(c.dve, lambda e: e.memset(c1f[:], 0.0), writes=[bv["c1f"]])
            c.op(c.dve, lambda e: e.memset(c1b[:], 0.0), writes=[bv["c1b"]])
            c.mm_group(bv["ps_states"], [(lambda e: e.matmul(
                ps_states[:, :], zcol[:], zrow[:], start=True, stop=False,
                skip_group_check=True), [bv["ones"]])])
        if ui + 1 < len(units):
            pending = unit_items(*units[ui + 1], (ui + 1) % 2)
        else:
            pending = []
        if not ilv:
            while pending:
                pending.pop(0)()
        else:
            tail = pending[ilv_n:]
            pending = pending[:ilv_n]
            for it in tail:
                pass
            self_tail = tail
        if phases == "scan_only_check":
            pending = []
        for tc in range(16):
            first = (h2 == 0 and tc == 0)
            last = (h2 == 1 and tc == 15)
            stA = chunk_stages(0, tc, first, last, bs)
            stB = chunk_stages(1, tc, first, last, bs)
            for s in range(6):
                if str(s) not in stmask:
                    continue
                stA[s]()
                stB[s]()
            for _ in range(5):
                if pending:
                    pending.pop(0)()
        while pending:
            pending.pop(0)()
        if ilv and 'self_tail' in dir():
            pass
        if ilv:
            for it in self_tail:
                it()
            self_tail = []
        for head in range(2):
            hb = head
            for w4 in range(4):
                w512 = slice(512 * w4, 512 * (w4 + 1))
                c.op(c.act, lambda e, w512=w512, hb=hb: e.activation(
                    out=onw[:], in_=oraw[hb][:, w512], func=AF.Square),
                    reads=[bv["oraw"][hb]], writes=[bv["onw"]])
                c.mm_group([bv["ps_in"][hb]], [(lambda e, hb=hb: e.matmul(
                    ps_in[hb][0:1, 0:512], ones_col[:], onw[:],
                    start=True, stop=True, skip_group_check=True),
                    [bv["ones"], bv["onw"]])])
                c.op(c.act, lambda e, w4=w4, hb=hb: e.activation(
                    out=rbw[0:1, 0:512],
                    in_=ps_in[hb][0:1, 0:512], func=AF.Ln,
                    scale=1.0 / DV, bias=eps_c),
                    reads=[bv["ps_in"][hb], c.const_buf], writes=[bv["rbw"]])
                c.op(c.act, lambda e, w4=w4, hb=hb: e.activation(
                    out=rnb[hb][0:1, 512 * w4:512 * (w4 + 1)],
                    in_=rbw[0:1, 0:512], func=AF.Exp, scale=-0.5),
                    reads=[bv["rbw"]], writes=[bv["rnb"][hb]])
            for w4 in range(4):
                w512 = slice(512 * w4, 512 * (w4 + 1))
                c.mm_group([bv["ps_in"][hb]], [(lambda e, w512=w512, hb=hb: e.matmul(
                    ps_in[hb][:, 0:512], gnwrb[:], rnb[hb][0:1, w512],
                    start=True, stop=True, skip_group_check=True),
                    [bv["gnws"], bv["rnb"][hb]])])
                c.op(c.dve, lambda e, hb=hb: e.tensor_copy(
                    out=rbw[:], in_=ps_in[hb][:, 0:512]),
                    reads=[bv["ps_in"][hb]], writes=[bv["rbw"]])
                c.op(c.dve, lambda e, w512=w512, hb=hb: e.tensor_tensor(
                    out=onw[:], in0=rbw[:], in1=oraw[hb][:, w512], op=OP.mult),
                    reads=[bv["rbw"], bv["oraw"][hb]], writes=[bv["onw"]])
                c.op(c.dve, lambda e, w512=w512, hb=hb, head=head, bs=bs:
                     e.tensor_tensor(
                        out=ogsb[:, head, w512], in0=onw[:],
                        in1=qkvs[bs][:, K_S, hslice(head, w512)], op=OP.mult),
                     reads=[bv["onw"], bv["qkvs"][bs]], writes=[bv["ogsb"]])
        for head in range(2):
            c.op(c.pool, lambda e, head=head, tbase=tbase: e.dma_start(
                out=ogo[128 * head:128 * (head + 1), tbase:tbase + 2048],
                in_=ogsb[:, head, :]), reads=[bv["ogsb"]], inc=16)

    c.finalize_queues()
    return c.build()



def _packed_T(w_packed):
    b = np.ascontiguousarray(w_packed).view(np.uint8).reshape(
        w_packed.shape[0], w_packed.shape[1], 4)[:, :, 0]
    return np.ascontiguousarray(b.T)


def _f32T(w):
    return np.ascontiguousarray(np.asarray(w, np.float32).T)


def _rope_tables():
    pos = np.arange(T, dtype=np.float32)
    inv = (10000.0 ** (-(np.arange(0, DK, 2, dtype=np.float32)) / DK))
    ang = inv[:, None] * pos[None, :]
    cos = np.cos(ang).astype(np.float32)
    sin = np.sin(ang).astype(np.float32)
    cosd = np.concatenate([cos, cos], axis=0)
    sins = np.concatenate([sin, -sin], axis=0)
    return cosd, sins


def _l2_const_inputs():
    cosd, sins = _rope_tables()
    triu = np.triu(np.ones((128, 128), np.float32)).astype(BF)
    idn = np.eye(128, dtype=np.float32).astype(BF)
    return cosd.astype(BF), sins.astype(BF), triu, idn


def _l2_maps(xln2_full, wq_b, wk_b, wv_b, wga_b, ws1_b, ws2_b, bs2, gnwv):
    cd, sd, triu, idn = _l2_const_inputs()
    bs2 = np.asarray(bs2, np.float32)
    maps = []
    for cid in range(N_CORES):
        hc = slice(256 * cid, 256 * (cid + 1))
        mc = slice(128 * cid, 128 * (cid + 1))
        maps.append({
            "xall": xln2_full,
            "wq": np.ascontiguousarray(wq_b[:, hc]),
            "wk": np.ascontiguousarray(wk_b[:, hc]),
            "wv": np.ascontiguousarray(wv_b[:, hc]),
            "wga": np.ascontiguousarray(wga_b[:, hc]),
            "ws1": ws1_b,
            "ws2": np.ascontiguousarray(ws2_b[:, mc]),
            "bs2A": np.ascontiguousarray(bs2[128 * cid:128 * cid + 64][:, None]),
            "bs2B": np.ascontiguousarray(bs2[128 * cid + 64:128 * cid + 128][:, None]),
            "bs2r": np.ascontiguousarray(bs2[mc][None, :]),
            "gnwr": np.ascontiguousarray(np.asarray(gnwv, np.float32)[None, :]),
            "cosd": cd, "sind": sd,
            "triu": triu, "idn": idn,
        })
    return maps


def kernel(
    L0_attn_hidden_states, L0_attn_Wq, L0_attn_Wk, L0_attn_Wv, L0_attn_Wg,
    L0_attn_Wo, L0_attn_Ws1, L0_attn_Ws2, L0_attn_bs2, L0_attn_g_norm_weight,
    L0_ffn_gate_W_packed, L0_ffn_gate_scales, L0_ffn_up_W_packed,
    L0_ffn_up_scales, L0_ffn_down_W_packed, L0_ffn_down_scales,
):
    pk_g = _packed_T(L0_ffn_gate_W_packed)
    pk_u = _packed_T(L0_ffn_up_W_packed)
    pk_d = _packed_T(L0_ffn_down_W_packed)
    sc_g = _f32T(L0_ffn_gate_scales)
    sc_u = _f32T(L0_ffn_up_scales)
    sc_d = _f32T(L0_ffn_down_scales)
    wqT = _f32T(L0_attn_Wq)
    wkT = _f32T(L0_attn_Wk)
    wvT = _f32T(L0_attn_Wv)
    wgaT = _f32T(L0_attn_Wg)
    woT = _f32T(L0_attn_Wo)
    ws1T = _f32T(L0_attn_Ws1)
    ws2T = _f32T(L0_attn_Ws2)
    p8 = np.zeros((8, 128), np.float32)
    for r in range(8):
        p8[r, 16 * r:16 * (r + 1)] = 1.0

    nc0 = build_l0()
    maps0 = []
    for cid in range(N_CORES):
        maps0.append({
            "pk_g": pk_g[128 * cid:128 * (cid + 1)],
            "pk_u": pk_u[128 * cid:128 * (cid + 1)],
            "pk_d": pk_d[1024 * cid:1024 * (cid + 1)],
            "sc_g": sc_g[8 * cid:8 * (cid + 1)],
            "sc_u": sc_u[8 * cid:8 * (cid + 1)],
            "sc_d": sc_d[64 * cid:64 * (cid + 1)],
            "p8": p8,
            "wq": wqT[256 * cid:256 * (cid + 1)],
            "wk": wkT[256 * cid:256 * (cid + 1)],
            "wv": wvT[256 * cid:256 * (cid + 1)],
            "wga": wgaT[256 * cid:256 * (cid + 1)],
            "wo": woT[256 * cid:256 * (cid + 1)],
            "ws1": ws1T,
            "ws2": ws2T,
        })
    res0 = run_bass_kernel_spmd(nc0, maps0, core_ids=list(range(N_CORES)))
    r0 = res0.results
    WgT = np.concatenate([r0[c]["o_g"] for c in range(N_CORES)], axis=0)
    WuT = np.concatenate([r0[c]["o_u"] for c in range(N_CORES)], axis=0)
    WdT = np.concatenate([r0[c]["o_d"] for c in range(N_CORES)], axis=0)
    wq_b = np.concatenate([r0[c]["owq"] for c in range(N_CORES)], axis=0)
    wk_b = np.concatenate([r0[c]["owk"] for c in range(N_CORES)], axis=0)
    wv_b = np.concatenate([r0[c]["owv"] for c in range(N_CORES)], axis=0)
    wga_b = np.concatenate([r0[c]["owga"] for c in range(N_CORES)], axis=0)
    wo_b = np.concatenate([r0[c]["owo"] for c in range(N_CORES)], axis=0)
    ws1_b = r0[0]["o_ws1"]
    ws2_b = r0[0]["o_ws2"]

    x0T = np.ascontiguousarray(
        np.asarray(L0_attn_hidden_states, np.float32).reshape(B * T, D).T)
    nc1 = build_ffn_launch(is_l3=False)
    maps1 = [{
        "xin": np.ascontiguousarray(x0T[:, TC * cid:TC * (cid + 1)]),
        "wgt": WgT, "wut": WuT, "wdt": WdT,
    } for cid in range(N_CORES)]
    res1 = run_bass_kernel_spmd(nc1, maps1, core_ids=list(range(N_CORES)))
    x1s = [res1.results[cid]["xout"] for cid in range(N_CORES)]
    xln2_full = np.concatenate(
        [res1.results[cid]["xlnout"] for cid in range(N_CORES)], axis=1)

    nc2 = build_l2()
    maps2 = _l2_maps(xln2_full, wq_b, wk_b, wv_b, wga_b, ws1_b, ws2_b,
                     L0_attn_bs2, L0_attn_g_norm_weight)
    res2 = run_bass_kernel_spmd(nc2, maps2, core_ids=list(range(N_CORES)))
    og_full = np.concatenate(
        [res2.results[cid]["ogo"] for cid in range(N_CORES)], axis=0)

    nc3 = build_ffn_launch(is_l3=True)
    maps3 = [{
        "xin": x1s[cid],
        "og": np.ascontiguousarray(og_full[:, TC * cid:TC * (cid + 1)]),
        "wot": wo_b,
        "wgt": WgT, "wut": WuT, "wdt": WdT,
    } for cid in range(N_CORES)]
    res3 = run_bass_kernel_spmd(nc3, maps3, core_ids=list(range(N_CORES)))
    x3 = np.concatenate(
        [res3.results[cid]["xout"] for cid in range(N_CORES)], axis=1)
    return np.ascontiguousarray(x3.T).reshape(B, T, D)
